# version 05
# baseline (speedup 1.0000x reference)
import time
import numpy as np
import concourse.bass as bass
import concourse.mybir as mybir
from concourse.bass_utils import run_bass_kernel_spmd
from contextlib import ExitStack

F32 = mybir.dt.float32
BF16 = mybir.dt.bfloat16
I32 = mybir.dt.int32
AF = mybir.ActivationFunctionType
ALU = mybir.AluOpType

H = 16
D = 128
HID = 2048
S = 4096
INTER = 5504
NIT = 43
NHC = 16
WIN = 410
RSQ = float(1.0 / np.sqrt(128.0))
BIG = 1.0e9
BISECT_ITERS = 42
HEAVY_K = 409.0
EPS = 1.0e-5
DECW = 704
DHALF = 2816
POLL_LIMIT = 200_000_000


class TSem:
    def __init__(self, h):
        self.h = h
        self.n = 0


class Emit:
    def __init__(self, eng, handles, sems):
        self.eng = eng
        self.h = handles
        self.sems = sems
        self._waited = {}

    def on(self, eng):
        return self.eng == eng

    def wait(self, eng, sem, val=None):
        if val is None:
            val = sem.n
        if val <= 0:
            return
        if self.eng == eng:
            key = id(sem)
            if self._waited.get(key, 0) < val:
                self.h[eng].wait_ge(sem.h, val)
                self._waited[key] = val

    def inc(self, eng, ins, sem, k=1):
        if self.eng == eng and ins is not None:
            ins.then_inc(sem.h, k)
        sem.n += k
        return sem.n


_NON_EMIT = {"wait_ge", "then_inc", "sem_clear", "engine_nop"}


class SelfSync:

    def __init__(self, h, tsem):
        self._h = h
        self._tsem = tsem

    def __getattr__(self, name):
        fn = getattr(self._h, name)
        if name.startswith("_") or name in _NON_EMIT or not callable(fn):
            return fn
        tsem = self._tsem
        h = self._h

        def wrapped(*a, **k):
            if tsem.n > 0:
                h.wait_ge(tsem.h, tsem.n)
            return fn(*a, **k)
        return wrapped


def build_consts():
    import ml_dtypes
    c = {}
    inv_freq = (1.0 / (10000.0 ** (np.arange(0, D, 2, dtype=np.float32) / np.float32(D)))).astype(np.float32)
    pos = np.arange(S, dtype=np.float32)
    freqs = pos[:, None] * inv_freq[None, :]
    emb = np.concatenate([freqs, freqs], axis=-1)
    c["cosbC"] = np.ascontiguousarray(np.cos(emb).astype(np.float32).T).astype(ml_dtypes.bfloat16)
    c["sinbC"] = np.ascontiguousarray(np.sin(emb).astype(np.float32).T).astype(ml_dtypes.bfloat16)
    R = np.zeros((D, D), dtype=np.float32)
    for i in range(D):
        if i < 64:
            R[i + 64, i] = -1.0
        else:
            R[i - 64, i] = 1.0
    c["rotC"] = R.astype(ml_dtypes.bfloat16)
    c["identC"] = np.eye(128, dtype=np.float32).astype(ml_dtypes.bfloat16)
    qi = np.arange(128)[:, None]
    ki = np.arange(128)[None, :]
    c["triC"] = (ki <= qi).astype(np.float32).astype(ml_dtypes.bfloat16)
    c["iota512C"] = np.broadcast_to(np.arange(512, dtype=np.float32)[None, :], (128, 512)).copy()
    c["iota128cC"] = np.arange(128, dtype=np.float32)[:, None].copy()
    c["onesrC"] = np.ones((1, 128), dtype=np.float32)
    return c


CONST_KEYS = ["cosbC", "sinbC", "rotC", "identC", "triC", "iota512C", "iota128cC", "onesrC"]

B_XB = 0
B_PA0 = 4096
B_PA1 = 4096
B_WQ = 8192
B_WK = 12288
B_WV = 16384
B_BIS = 20480
B_DEC = 21504
B_PCP = 24320
B_PCO = 25344
B_QSB = 25856
B_M32 = 26368
B_COS = 26496
B_SIN = 27008
B_END = 27712
B_XFB = 0
B_AT = 16384
B_WO = 24576
B_GU = 16384
B_WD = 16384

C_ACC = 0
C_XF2 = 8192
C_MEANB = 9216
C_RSTDB = 9728
C_TMP = 10240
C_MASKT = 10752
C_ZPART = 10880
C_LUTB = 10896
C_END = 11008

A_HH = 0
A_LNB = 44032
A_LN2B = 44544
A_SG = 45056
A_END = 49152


def build_fused():
    nc = bass.Bass("TRN2", target_bir_lowering=False, debug=False, dynamic_dma_scratch_size=16384)
    dt = nc.dram_tensor
    xIm = dt("xIm", [128, 8 * NHC * 512], F32, kind="ExternalInput").ap()
    wqT = dt("wqT", [128, 4 * 4096], F32, kind="ExternalInput").ap()
    wkT = dt("wkT", [128, 4 * 4096], F32, kind="ExternalInput").ap()
    wvT = dt("wvT", [128, 4 * 4096], F32, kind="ExternalInput").ap()
    woT = dt("woT", [HID, HID], F32, kind="ExternalInput").ap()
    xTs = dt("xTs", [HID, 1024], F32, kind="ExternalInput").ap()
    gidxT = dt("gidxT", [64 * 128, DECW], I32, kind="ExternalInput").ap()
    uidxT = dt("uidxT", [64 * 128, DECW], I32, kind="ExternalInput").ap()
    didxT = dt("didxT", [66 * 128, DECW], I32, kind="ExternalInput").ap()
    lut16 = dt("lut16", [1, 16], F32, kind="ExternalInput").ap()
    myslot = dt("myslot", [1, 1], I32, kind="ExternalInput").ap()
    qsel = dt("qsel", [1, 1], I32, kind="ExternalInput").ap()
    nonce = dt("nonce", [1, 2], I32, kind="ExternalInput").ap()
    cosb_d = dt("cosbC", [D, S], BF16, kind="ExternalInput").ap()
    sinb_d = dt("sinbC", [D, S], BF16, kind="ExternalInput").ap()
    rot_d = dt("rotC", [D, D], BF16, kind="ExternalInput").ap()
    ident_d = dt("identC", [128, 128], BF16, kind="ExternalInput").ap()
    tri_d = dt("triC", [128, 128], BF16, kind="ExternalInput").ap()
    iota512_d = dt("iota512C", [128, 512], F32, kind="ExternalInput").ap()
    iota128c_d = dt("iota128cC", [128, 1], F32, kind="ExternalInput").ap()
    onesr_d = dt("onesrC", [1, 128], F32, kind="ExternalInput").ap()

    outT_o = dt("outT", [HID, 1024], F32, kind="ExternalOutput").ap()
    import os as _osd
    DBG = _osd.environ.get("FUSED_DEBUG", "")
    if DBG:
        dbgA_o = dt("dbgA", [1024, S], BF16, kind="ExternalOutput").ap()
        dbgB_o = dt("dbgB", [1024, S], BF16, kind="ExternalOutput").ap()
        dbgX_o = dt("dbgX", [128, 16384], BF16, kind="ExternalOutput").ap()

    tsum_d = dt("tsum_d", [8, S], F32)
    mask_d = dt("mask_d", [8, S], BF16)
    sh_atA = dt("sh_atA", [1024, S], BF16, addr_space="Shared")
    sh_atB = dt("sh_atB", [1024, S], BF16, addr_space="Shared")
    sh_wg = dt("sh_wg", [HID, INTER], BF16, addr_space="Shared")
    sh_wu = dt("sh_wu", [HID, INTER], BF16, addr_space="Shared")
    sh_wd = dt("sh_wd", [INTER, HID], BF16, addr_space="Shared")
    shf = dt("shf", [2, 16], I32, addr_space="Shared")

    import os
    STG = int(os.environ.get("FUSED_STAGES", "9"))
    PLIMIT = int(os.environ.get("FUSED_POLL_LIMIT", str(POLL_LIMIT)))

    with ExitStack() as ctx:
        e = ctx.enter_context
        AR_A = e(nc.sbuf_tensor("AR_A", [128, A_END], BF16))
        AR_B = e(nc.sbuf_tensor("AR_B", [128, B_END], BF16))
        AR_C = e(nc.sbuf_tensor("AR_C", [128, C_END], F32))
        rot_s = e(nc.sbuf_tensor("rot_s", [D, D], BF16))
        identb_s = e(nc.sbuf_tensor("identb_s", [128, 128], BF16))
        tri_s = e(nc.sbuf_tensor("tri_s", [128, 128], BF16))
        onesrF_s = e(nc.sbuf_tensor("onesrF_s", [1, 128], F32))
        onescb_s = e(nc.sbuf_tensor("onescb_s", [128, 1], BF16))
        lutrow_s = e(nc.sbuf_tensor("lutrow_s", [1, 16], F32))
        iota128c_s = e(nc.sbuf_tensor("iota128c_s", [128, 1], F32))
        zsum_s = e(nc.sbuf_tensor("zsum_s", [128, 1], F32))
        rinv_s = e(nc.sbuf_tensor("rinv_s", [128, 1], F32))
        rinvb_s = e(nc.sbuf_tensor("rinvb_s", [128, 1], BF16))
        bis_lo = e(nc.sbuf_tensor("bis_lo", [128, 1], F32))
        bis_hi = e(nc.sbuf_tensor("bis_hi", [128, 1], F32))
        bis_mid = e(nc.sbuf_tensor("bis_mid", [128, 1], F32))
        bis_cnt = e(nc.sbuf_tensor("bis_cnt", [128, 1], F32))
        bis_pred = e(nc.sbuf_tensor("bis_pred", [128, 1], F32))
        bis_d = e(nc.sbuf_tensor("bis_d", [128, 1], F32))
        bis_t = e(nc.sbuf_tensor("bis_t", [128, 1], F32))
        pcS_s = e(nc.sbuf_tensor("pcS_s", [128, 1], F32))
        pcS410_s = e(nc.sbuf_tensor("pcS410_s", [128, 1], F32))
        pcE_s = e(nc.sbuf_tensor("pcE_s", [128, 1], F32))
        mean_s = e(nc.sbuf_tensor("mean_s", [1, 512], F32))
        ex2_s = e(nc.sbuf_tensor("ex2_s", [1, 512], F32))
        var_s = e(nc.sbuf_tensor("var_s", [1, 512], F32))
        sq_s = e(nc.sbuf_tensor("sq_s", [1, 512], F32))
        rstd_s = e(nc.sbuf_tensor("rstd_s", [1, 512], F32))
        pcr_s = e(nc.sbuf_tensor("pcr_s", [1, 512], F32))
        sslot_s = e(nc.sbuf_tensor("sslot_s", [1, 1], I32))
        sqsel_s = e(nc.sbuf_tensor("sqsel_s", [1, 1], I32))
        snon_s = e(nc.sbuf_tensor("snon_s", [1, 2], I32))

        ps = [e(nc.psum_tensor(f"ps{i}", [128, 512], F32)) for i in range(7)]
        p_trb = e(nc.psum_tensor("p_trb", [128, 128], BF16))

        sems = {n: TSem(e(nc.semaphore(n)))
                for n in ["s_ld", "s_x", "s_mm", "s_act", "s_dve", "s_st",
                          "s_gp", "s_dk", "s_gm", "s_ga", "s_w0", "s_w1",
                          "s_xb", "s_wt", "s_fl", "s_x0", "s_x1", "s_f0",
                          "s_f1", "s_bar"]}
        block = e(nc.Block())

        qT4 = [AR_A[:, (4 + i) * 4096:(5 + i) * 4096] for i in range(4)]
        kT4 = [AR_A[:, i * 4096:(i + 1) * 4096] for i in range(4)]
        vN4 = [AR_A[:, (8 + i) * 4096:(9 + i) * 4096] for i in range(4)]
        hh_s = AR_A[:, A_HH:A_HH + NIT * 1024]
        lnb_s = AR_A[:, A_LNB:A_LNB + 512]
        ln2b_s = AR_A[:, A_LN2B:A_LN2B + 512]
        sg_s = AR_A[:, A_SG:A_SG + 512]

        xb_s = AR_B[:, B_XB:B_XB + 4096]
        pA_s = [AR_B[:, B_PA0:B_PA0 + 4096], AR_B[:, B_PA0:B_PA0 + 4096]]
        wq_s = AR_B[:, B_WQ:B_WQ + 4096]
        wk_s = AR_B[:, B_WK:B_WK + 4096]
        wv_s = AR_B[:, B_WV:B_WV + 4096]
        bis_scrap = AR_B[:, B_BIS:B_BIS + 1024]
        dec_idx = [AR_B[:, B_DEC + i * DECW:B_DEC + (i + 1) * DECW] for i in range(2)]
        dec_acc = [AR_B[:, B_DEC + 2 * DECW:B_DEC + 3 * DECW] for i in range(2)]
        dec_tmp = AR_B[:, B_DEC + 3 * DECW:B_DEC + 4 * DECW]
        pcP_s = [AR_B[:, B_PCP + i * 512:B_PCP + (i + 1) * 512] for i in range(2)]
        pcout_s = [AR_B[:, B_PCO:B_PCO + 512] for i in range(2)]
        qsb_s = AR_B[:, B_QSB:B_QSB + 512]
        m32_s = AR_B[:, B_M32:B_M32 + 128]
        cos_s = AR_B[:, B_COS:B_COS + 512]
        sin_s = AR_B[:, B_SIN:B_SIN + 512]
        rotb_s = AR_A[:, A_SG:A_SG + 512]
        xfb_s = AR_B[:, B_XFB:B_XFB + 16384]
        at_s = AR_B[:, B_AT:B_AT + 8192]
        wo_s = [AR_B[:, B_WO + i * 512:B_WO + (i + 1) * 512] for i in range(2)]
        gu_s = [AR_B[:, B_GU + i * 4096:B_GU + (i + 1) * 4096] for i in range(2)]
        wd_s = [AR_B[:, B_WD + i * 5632:B_WD + (i + 1) * 5632] for i in range(2)]

        acc_s = AR_C[:, C_ACC:C_ACC + 8192]
        bis_xf = AR_C[:, C_ACC:C_ACC + 4096]
        vpart_s = AR_C[:, C_ACC + 4096:C_ACC + 5120]
        xf2 = [AR_C[:, C_XF2 + i * 512:C_XF2 + (i + 1) * 512] for i in range(2)]
        rope_t1 = AR_C[:, C_XF2:C_XF2 + 512]
        rope_t2 = AR_C[:, C_XF2 + 512:C_XF2 + 1024]
        meanB_s = AR_C[:, C_MEANB:C_MEANB + 512]
        tstage_s = AR_C[:, C_MEANB:C_MEANB + 512]
        rstdB_s = AR_C[:, C_RSTDB:C_RSTDB + 512]
        pcrB_s = AR_C[:, C_RSTDB:C_RSTDB + 512]
        tmp_s = AR_C[:, C_TMP:C_TMP + 512]
        iota512_s = AR_C[:, C_TMP:C_TMP + 512]
        maskT_s = AR_C[:, C_MASKT:C_MASKT + 128]
        zpart_s = AR_C[:, C_ZPART:C_ZPART + 16]
        lutb_s = AR_C[:, C_LUTB:C_LUTB + 16]

        state = {}
        REG = {}

        def schedule(em: Emit):
            state.clear()
            REG.clear()
            SM = sems
            t = em.h.get("t")
            v = em.h.get("v")
            s = em.h.get("s")
            g = em.h.get("g")
            q = em.h.get("q")

            for dst, src in [(iota128c_s[:, :], iota128c_d), (lutrow_s[0:1, :], lut16),
                             (onesrF_s[0:1, :], onesr_d), (sslot_s[0:1, :], myslot),
                             (sqsel_s[0:1, :], qsel), (snon_s[0:1, :], nonce)]:
                ins = q.dma_start(out=dst, in_=src) if em.on("q") else None
                em.inc("q", ins, SM["s_ld"], 16)
            ins = q.dma_start(out=iota512_s, in_=iota512_d) if em.on("q") else None
            lv0 = em.inc("q", ins, SM["s_ld"], 16)
            for dst, srcc in [(rot_s[:, :], rot_d), (identb_s[:, :], ident_d), (tri_s[:, :], tri_d)]:
                ins = q.dma_start(out=dst, in_=srcc) if em.on("q") else None
                em.inc("q", ins, SM["s_ld"], 16)
            if em.on("g"):
                for rn in ["slot", "qs", "nonr", "flag", "cond", "c2", "itr"]:
                    REG[rn] = g.alloc_register(rn)
                g.wait_ge(SM["s_ld"].h, lv0)
                g.load(REG["slot"], sslot_s[0:1, 0:1])
                g.load(REG["qs"], sqsel_s[0:1, 0:1])
            em._waited[id(SM["s_ld"])] = max(em._waited.get(id(SM["s_ld"]), 0), lv0 if em.on("g") else 0)
            ins = v.memset(onescb_s[:, :], 1.0) if em.on("v") else None
            em.inc("v", ins, SM["s_dve"])
            ins = v.memset(m32_s, 0.0) if em.on("v") else None
            em.inc("v", ins, SM["s_dve"])
            ins = v.memset(bis_xf[:, :], 0.0) if em.on("v") else None
            em.inc("v", ins, SM["s_dve"])
            em.wait("t", SM["s_ld"])
            ins = t.matmul(ps[5][:, 0:16], onesrF_s[:, :], lutrow_s[0:1, :], start=True, stop=True) if em.on("t") else None
            em.inc("t", ins, SM["s_mm"])
            em.wait("s", SM["s_mm"])
            ins = s.copy(lutb_s, ps[5][:, 0:16]) if em.on("s") else None
            lut_ready = em.inc("s", ins, SM["s_act"])

            def gbr(mk0, mk1, sem, k=16):
                if em.on("g"):
                    with g.If_eq(REG["slot"], 0):
                        mk0().then_inc(sem.h, k)
                    with g.Else():
                        mk1().then_inc(sem.h, k)
                sem.n += k
                return sem.n

            dec_jobs = []
            NODEC = os.environ.get("FUSED_NODEC") == "1"
            for srcd, dstd in [] if NODEC else [(gidxT, sh_wg), (uidxT, sh_wu)]:
                for hc in range(NHC):
                    for ci, cc in enumerate(range(0, DHALF, DECW)):
                        dec_jobs.append((srcd, hc * 4 + ci, dstd, hc * 128, cc, DECW, False))
            for rt in [] if NODEC else range(22):
                for ci, (cc, cols) in enumerate([(0, DECW), (DECW, DECW), (2 * DECW, HID - 2 * DECW)]):
                    dec_jobs.append((didxT, rt * 3 + ci, sh_wd, rt * 128, cc, cols, True))
            dec_buf_free = [0, 0]

            def decode_ops():
                for j, (srcd, jrow, dstd, r0, c0, cols, is_down) in enumerate(dec_jobs):
                    buf = dec_idx[j % 2]
                    acc = dec_acc[j % 2]
                    rows = 128

                    def op_load(j=j, srcd=srcd, jrow=jrow, buf=buf):
                        em.wait("g", SM["s_gm"])
                        ins = g.dma_start(out=buf[:, 0:DECW],
                                          in_=srcd[jrow * 128:(jrow + 1) * 128, :]) if em.on("g") else None
                        lv = em.inc("g", ins, SM["s_dk"], 16)
                        state[("dk", j)] = lv
                    yield op_load

                    def op_first(j=j, buf=buf, acc=acc):
                        em.wait("v", SM["s_dk"], state[("dk", j)])
                        em.wait("v", SM["s_act"], lut_ready)
                        em.wait("v", SM["s_gm"], max(dec_buf_free[0], dec_buf_free[1]))
                        ins = v.tensor_scalar(acc[:, 0:DECW], buf[:, 0:DECW], 0.0,
                                              lutb_s[:, 0:1], ALU.is_equal, ALU.mult) if em.on("v") else None
                        em.inc("v", ins, SM["s_dve"])
                    yield op_first

                    for tv in range(1, 16):
                        def op_sel(tv=tv, buf=buf):
                            ins = v.tensor_scalar(dec_tmp[:, 0:DECW], buf[:, 0:DECW], float(tv),
                                                  lutb_s[:, tv:tv + 1], ALU.is_equal, ALU.mult) if em.on("v") else None
                            em.inc("v", ins, SM["s_dve"])
                        yield op_sel

                        def op_add(acc=acc):
                            ins = v.tensor_add(acc[:, 0:DECW], acc[:, 0:DECW],
                                               dec_tmp[:, 0:DECW]) if em.on("v") else None
                            em.inc("v", ins, SM["s_dve"])
                        yield op_add

                    def op_store(j=j, dstd=dstd, r0=r0, c0=c0, cols=cols,
                                 acc=acc, is_down=is_down):
                        em.wait("g", SM["s_dve"])
                        if is_down:
                            sv = gbr(lambda: g.dma_start(out=dstd.ap()[r0:r0 + 128, c0:c0 + cols],
                                                         in_=acc[:, 0:cols]),
                                     lambda: g.dma_start(out=dstd.ap()[2688 + r0:2688 + r0 + 128, c0:c0 + cols],
                                                         in_=acc[:, 0:cols]),
                                     SM["s_gm"])
                        else:
                            sv = gbr(lambda: g.dma_start(out=dstd.ap()[r0:r0 + 128, c0:c0 + cols],
                                                         in_=acc[:, 0:cols]),
                                     lambda: g.dma_start(out=dstd.ap()[r0:r0 + 128, 2688 + c0:2688 + c0 + cols],
                                                         in_=acc[:, 0:cols]),
                                     SM["s_gm"])
                        dec_buf_free[j % 2] = sv
                    yield op_store

            dec_stream = decode_ops()
            dec_done = [False]

            def emit_decode(n):
                for _ in range(n):
                    if dec_done[0]:
                        return
                    op = next(dec_stream, None)
                    if op is None:
                        dec_done[0] = True
                        return
                    op()

            def barrier(which):
                colf = [(0, 8), (4, 12)][which]
                gbr(lambda: g.dma_start(out=shf.ap()[0:1, colf[1]:colf[1] + 1], in_=snon_s[0:1, 1:2]),
                    lambda: g.dma_start(out=shf.ap()[1:2, colf[1]:colf[1] + 1], in_=snon_s[0:1, 1:2]),
                    SM["s_fl"])
                em.wait("g", SM["s_fl"])
                gbr(lambda: g.dma_start(out=shf.ap()[0:1, colf[0]:colf[0] + 1], in_=snon_s[0:1, 0:1]),
                    lambda: g.dma_start(out=shf.ap()[1:2, colf[0]:colf[0] + 1], in_=snon_s[0:1, 0:1]),
                    SM["s_fl"])
                em.wait("g", SM["s_fl"])
                if em.on("g"):
                    for word, col in [(0, colf[0]), (1, colf[1])]:
                        g.load(REG["nonr"], snon_s[0:1, word:word + 1])
                        g.reg_mov(REG["cond"], 1)
                        g.reg_mov(REG["itr"], 0)
                        with g.If_eq(REG["slot"], 0):
                            with g.While(REG["cond"]):
                                g.load(REG["flag"], shf.ap()[1:2, col:col + 1])
                                g.reg_alu(REG["cond"], REG["flag"], REG["nonr"], ALU.not_equal)
                                g.reg_add(REG["itr"], REG["itr"], 1)
                                g.reg_alu(REG["c2"], REG["itr"], PLIMIT, ALU.is_lt)
                                g.reg_alu(REG["cond"], REG["cond"], REG["c2"], ALU.logical_and)
                        with g.Else():
                            with g.While(REG["cond"]):
                                g.load(REG["flag"], shf.ap()[0:1, col:col + 1])
                                g.reg_alu(REG["cond"], REG["flag"], REG["nonr"], ALU.not_equal)
                                g.reg_add(REG["itr"], REG["itr"], 1)
                                g.reg_alu(REG["c2"], REG["itr"], PLIMIT, ALU.is_lt)
                                g.reg_alu(REG["cond"], REG["cond"], REG["c2"], ALU.logical_and)
                    g.sem_inc(SM["s_bar"].h, 1)
                SM["s_bar"].n += 1

            def proj_pair(gp):
                pl = gp % 2
                em.wait("t", SM["s_dve"])
                em.wait("t", SM["s_act"])
                for wsb, wsrc in [(wq_s, wqT), (wk_s, wkT), (wv_s, wvT)]:
                    em.wait("g", SM["s_mm"], state.get("wfree", 0))
                    ins = g.dma_start(out=wsb,
                                      in_=wsrc[:, gp * 4096:(gp + 1) * 4096]) if em.on("g") else None
                    em.inc("g", ins, SM["s_wt"], 16)
                wt_ready = SM["s_wt"].n
                for tc in range(8):
                    em.wait("q", SM["s_dve"], state.get("cs_free", 0))
                    ins = q.dma_start(out=cos_s, in_=cosb_d[:, tc * 512:(tc + 1) * 512]) if em.on("q") else None
                    em.inc("q", ins, SM["s_x"], 16)
                    ins = q.dma_start(out=sin_s, in_=sinb_d[:, tc * 512:(tc + 1) * 512]) if em.on("q") else None
                    cs_ld = em.inc("q", ins, SM["s_x"], 16)
                    for half in range(2):
                        for quarter in range(2):
                            hcbase = half * 8 + quarter * 4
                            em.wait("g", SM["s_mm"], state.get(("xbq", quarter), 0))
                            ins = g.dma_start(out=xb_s[:, quarter * 2048:(quarter + 1) * 2048],
                                              in_=xIm[:, tc * 8192 + hcbase * 512:
                                                      tc * 8192 + hcbase * 512 + 2048]) if em.on("g") else None
                            xlv = em.inc("g", ins, SM["s_xb"], 16)
                            em.wait("t", SM["s_xb"], xlv)
                            em.wait("t", SM["s_wt"], wt_ready)
                            for hh in range(2):
                                for wi, wsb in [(0, wq_s), (1, wk_s)]:
                                    pp = ps[wi * 2 + hh]
                                    ins = None
                                    for hq in range(4):
                                        hc = hcbase + hq
                                        ins = t.matmul(pp[:, :],
                                                       wsb[:, hc * 256 + hh * 128: hc * 256 + hh * 128 + 128],
                                                       xb_s[:, (quarter * 4 + hq) * 512:(quarter * 4 + hq) * 512 + 512],
                                                       start=(hc == 0), stop=(hc == NHC - 1)) if em.on("t") else None
                                    em.inc("t", ins, SM["s_mm"])
                        for tp in range(2):
                            bank = 4 + tp
                            em.wait("t", SM["s_act"] if half == 1 else SM["s_dve"],
                                    state.get(("vfree", bank), 0))
                            ins = None
                            for tt in (2 * tp, 2 * tp + 1):
                                cr = (tt % 2) * 256
                                for hh in range(2):
                                    for quarter in range(2):
                                        for hq in range(4):
                                            hc = half * 8 + quarter * 4 + hq
                                            ins = t.matmul(ps[bank][:, cr + hh * 128: cr + hh * 128 + 128],
                                                           xb_s[:, (quarter * 4 + hq) * 512 + tt * 128:
                                                                (quarter * 4 + hq) * 512 + tt * 128 + 128],
                                                           wv_s[:, hc * 256 + hh * 128: hc * 256 + hh * 128 + 128],
                                                           start=(quarter == 0 and hq == 0),
                                                           stop=(quarter == 1 and hq == 3)) if em.on("t") else None
                            state[("vmm", half, tp)] = em.inc("t", ins, SM["s_mm"])
                        state[("xbq", 0)] = SM["s_mm"].n
                        state[("xbq", 1)] = SM["s_mm"].n
                        if half == 0:
                            for tp in range(2):
                                bank = 4 + tp
                                em.wait("s", SM["s_mm"], state[("vmm", half, tp)])
                                em.wait("s", SM["s_dve"], state.get("vpart_free", 0))
                                ins = s.copy(vpart_s[:, tp * 512:(tp + 1) * 512],
                                             ps[bank][:, 0:512]) if em.on("s") else None
                                state[("vfree", bank)] = em.inc("s", ins, SM["s_act"])
                        else:
                            for tp in range(2):
                                bank = 4 + tp
                                em.wait("v", SM["s_mm"], state[("vmm", half, tp)])
                                em.wait("v", SM["s_act"])
                                for tt in (2 * tp, 2 * tp + 1):
                                    cr = (tt % 2) * 256
                                    for hh in range(2):
                                        slot = pl * 2 + hh
                                        ins = v.tensor_add(
                                            vN4[slot][:, (tc * 4 + tt) * 128:(tc * 4 + tt) * 128 + 128],
                                            ps[bank][:, cr + hh * 128:cr + hh * 128 + 128],
                                            vpart_s[:, tt * 256 + hh * 128:tt * 256 + hh * 128 + 128]) if em.on("v") else None
                                        state[("vfree", bank)] = em.inc("v", ins, SM["s_dve"])
                            state["vpart_free"] = SM["s_dve"].n
                    if DBG == "qkv" and gp == 1 and tc == 7:
                        em.wait("s", SM["s_mm"])
                        for bi in range(4):
                            ins = s.copy(AR_C[:, bi * 512:(bi + 1) * 512], ps[bi][:, 0:512]) if em.on("s") else None
                            em.inc("s", ins, SM["s_act"])
                    em.wait("v", SM["s_x"], cs_ld)
                    for hh in range(2):
                        slot = pl * 2 + hh
                        for wi, dstT in [(0, qT4), (1, kT4)]:
                            pp = ps[wi * 2 + hh]
                            em.wait("s", SM["s_mm"])
                            em.wait("s", SM["s_mm"], state.get("qsb_free", 0))
                            em.wait("s", SM["s_dve"], state.get("qsb_free_dve", 0))
                            ins = s.copy(qsb_s, pp[:, :]) if em.on("s") else None
                            ev = em.inc("s", ins, SM["s_act"])
                            em.wait("t", SM["s_act"], ev)
                            em.wait("t", SM["s_dve"], state.get("rot_free", 0))
                            ins = t.matmul(ps[6][:, :], rot_s[:, :], qsb_s,
                                           start=True, stop=True) if em.on("t") else None
                            rot_done = em.inc("t", ins, SM["s_mm"])
                            state["qsb_free"] = rot_done
                            em.wait("v", SM["s_mm"], rot_done)
                            ins = v.tensor_mul(rope_t1, ps[6][:, :], sin_s) if em.on("v") else None
                            em.inc("v", ins, SM["s_dve"])
                            ins = v.tensor_mul(rope_t2, pp[:, :], cos_s) if em.on("v") else None
                            em.inc("v", ins, SM["s_dve"])
                            ins = v.tensor_add(dstT[slot][:, tc * 512:(tc + 1) * 512],
                                               rope_t1, rope_t2) if em.on("v") else None
                            rope_done = em.inc("v", ins, SM["s_dve"])
                            state["rot_free"] = rope_done
                            state["qsb_free_dve"] = rope_done
                            em.wait("t", SM["s_dve"], rope_done)
                            if DBG == "rope" and gp == 0 and tc == 0 and hh == 0 and wi == 0:
                                em.wait("s", SM["s_dve"], rope_done)
                                ins = s.copy(AR_C[:, 0:512], qsb_s) if em.on("s") else None
                                em.inc("s", ins, SM["s_act"])
                                ins = s.copy(AR_C[:, 512:1024], rotb_s) if em.on("s") else None
                                em.inc("s", ins, SM["s_act"])
                                ins = v.tensor_copy(AR_C[:, 1024:1536], rope_t1) if em.on("v") else None
                                em.inc("v", ins, SM["s_dve"])
                                ins = v.tensor_copy(AR_C[:, 1536:2048], rope_t2) if em.on("v") else None
                                em.inc("v", ins, SM["s_dve"])
                                ins = s.copy(AR_C[:, 2048:2560], ps[wi * 2 + hh][:, 0:512]) if em.on("s") else None
                                em.inc("s", ins, SM["s_act"])
                    state["cs_free"] = SM["s_dve"].n
                    emit_decode(40)
                state["wfree"] = SM["s_mm"].n

            cs_banks = [2, 3, 4]
            pa_free = {0: 0, 1: 0}

            def pass_a(ghead):
                hd = ghead % 4
                em.wait("t", SM["s_dve"])
                b = None
                for qt in range(32):
                    b = qt % 2
                    kmax = (qt + 1) * 128
                    pieces = []
                    cpos = 0
                    while cpos < kmax - 128:
                        w = min(512, kmax - 128 - cpos)
                        pieces.append((cpos, w, False))
                        cpos += w
                    pieces.append((kmax - 128, 128, True))
                    em.wait("s", SM["s_mm"], max(pa_free[0], pa_free[1]))
                    npc = len(pieces)
                    for pi, (c0, w, is_diag) in enumerate(pieces):
                        bank = ps[pi % 2]
                        em.wait("t", SM["s_act"], state.get(("scfree", pi % 2), 0))
                        ins = t.matmul(bank[:, 0:w], qT4[hd][:, qt * 128:(qt + 1) * 128],
                                       kT4[hd][:, c0:c0 + w], start=True, stop=True) if em.on("t") else None
                        smm = em.inc("t", ins, SM["s_mm"])
                        em.wait("s", SM["s_mm"], smm)
                        if not is_diag:
                            ins = s.activation(pA_s[b][:, c0:c0 + w], bank[:, 0:w], AF.Exp, scale=RSQ,
                                               accum_out=zpart_s[:, pi:pi + 1]) if em.on("s") else None
                            sa = em.inc("s", ins, SM["s_act"])
                        else:
                            ins = s.activation(pA_s[b][:, c0:c0 + w], bank[:, 0:w], AF.Exp, scale=RSQ) if em.on("s") else None
                            sa = em.inc("s", ins, SM["s_act"])
                            em.wait("v", SM["s_act"], sa)
                            ins = v.scalar_tensor_tensor(pA_s[b][:, c0:c0 + w], tri_s[:, :], 1.0,
                                                         pA_s[b][:, c0:c0 + w], ALU.mult, ALU.mult,
                                                         accum_out=zpart_s[:, pi:pi + 1]) if em.on("v") else None
                            em.inc("v", ins, SM["s_dve"])
                        state[("scfree", pi % 2)] = sa
                    em.wait("v", SM["s_act"])
                    ins = v.reduce_sum(zsum_s[:, 0:1], zpart_s[:, 0:npc], axis=mybir.AxisListType.X) if em.on("v") else None
                    em.inc("v", ins, SM["s_dve"])
                    ins = v.reciprocal(rinv_s[:, 0:1], zsum_s[:, 0:1]) if em.on("v") else None
                    em.inc("v", ins, SM["s_dve"])
                    ins = v.tensor_copy(rinvb_s[:, 0:1], rinv_s[:, 0:1]) if em.on("v") else None
                    rv = em.inc("v", ins, SM["s_dve"])
                    kpad = ((kmax + 511) // 512) * 512
                    if kpad > kmax:
                        em.wait("v", SM["s_mm"], max(pa_free[0], pa_free[1]))
                        ins = v.memset(pA_s[b][:, kmax:kpad], 0.0) if em.on("v") else None
                        rv = em.inc("v", ins, SM["s_dve"])
                    em.wait("t", SM["s_dve"], rv)
                    ins = None
                    for ck in range(kpad // 512):
                        first = (qt == 4 * ck)
                        ins = t.matmul(ps[cs_banks[ck % 3]][32 * (ck // 3):32 * (ck // 3) + 1, 0:512],
                                       rinvb_s[:, 0:1], pA_s[b][:, 512 * ck:512 * ck + 512],
                                       start=first, stop=(qt == 31)) if em.on("t") else None
                    pa_free[b] = em.inc("t", ins, SM["s_mm"])
                    if qt % 4 == 3:
                        emit_decode(8)
                em.wait("v", SM["s_mm"], pa_free[b])
                for ck in range(8):
                    r0 = 32 * (ck // 3)
                    em.wait("v", SM["s_gp"])
                    ins = v.tensor_copy(tstage_s[r0:r0 + 1, 0:512],
                                        ps[cs_banks[ck % 3]][r0:r0 + 1, 0:512]) if em.on("v") else None
                    dvv = em.inc("v", ins, SM["s_dve"])
                    em.wait("q", SM["s_dve"], dvv)
                    ins = q.dma_start(out=tsum_d.ap()[ghead:ghead + 1, 512 * ck:512 * ck + 512],
                                      in_=tstage_s[r0:r0 + 1, 0:512]) if em.on("q") else None
                    em.inc("q", ins, SM["s_gp"], 16)

            def bisect_begin(p):
                em.wait("q", SM["s_gp"])
                em.wait("q", SM["s_dve"])
                ins = q.dma_start(out=bis_xf[0:1, 0:S], in_=tsum_d.ap()[2 * p:2 * p + 1, :]) if em.on("q") else None
                em.inc("q", ins, SM["s_x"], 16)
                ins = q.dma_start(out=bis_xf[64:65, 0:S], in_=tsum_d.ap()[2 * p + 1:2 * p + 2, :]) if em.on("q") else None
                lv = em.inc("q", ins, SM["s_x"], 16)
                em.wait("v", SM["s_x"], lv)
                ins = v.memset(bis_lo[:, :], 0.0) if em.on("v") else None
                em.inc("v", ins, SM["s_dve"])
                ins = v.memset(bis_hi[:, :], 4096.0) if em.on("v") else None
                em.inc("v", ins, SM["s_dve"])

            def bisect_iter():
                ins = v.tensor_add(bis_mid[:, :], bis_lo[:, :], bis_hi[:, :]) if em.on("v") else None
                em.inc("v", ins, SM["s_dve"])
                ins = v.tensor_scalar(bis_mid[:, :], bis_mid[:, :], 0.5, 0.0, ALU.mult, ALU.add) if em.on("v") else None
                em.inc("v", ins, SM["s_dve"])
                for bq in range(4):
                    ins = v.tensor_scalar(bis_scrap[:, 0:1024], bis_xf[:, bq * 1024:(bq + 1) * 1024],
                                          bis_mid[:, 0:1], 0.0, ALU.is_gt, ALU.add,
                                          accum_out=(bis_cnt if bq == 0 else bis_t)[:, 0:1]) if em.on("v") else None
                    em.inc("v", ins, SM["s_dve"])
                    if bq > 0:
                        ins = v.tensor_add(bis_cnt[:, :], bis_cnt[:, :], bis_t[:, :]) if em.on("v") else None
                        em.inc("v", ins, SM["s_dve"])
                ins = v.tensor_scalar(bis_pred[:, :], bis_cnt[:, :], HEAVY_K, 1.0,
                                      ALU.is_ge, ALU.mult) if em.on("v") else None
                em.inc("v", ins, SM["s_dve"])
                ins = v.tensor_sub(bis_d[:, :], bis_mid[:, :], bis_lo[:, :]) if em.on("v") else None
                em.inc("v", ins, SM["s_dve"])
                ins = v.tensor_mul(bis_t[:, :], bis_pred[:, :], bis_d[:, :]) if em.on("v") else None
                em.inc("v", ins, SM["s_dve"])
                ins = v.tensor_add(bis_lo[:, :], bis_lo[:, :], bis_t[:, :]) if em.on("v") else None
                em.inc("v", ins, SM["s_dve"])
                ins = v.tensor_scalar(bis_d[:, :], bis_pred[:, :], -1.0, 1.0, ALU.mult, ALU.add) if em.on("v") else None
                em.inc("v", ins, SM["s_dve"])
                ins = v.tensor_sub(bis_t[:, :], bis_mid[:, :], bis_hi[:, :]) if em.on("v") else None
                em.inc("v", ins, SM["s_dve"])
                ins = v.tensor_mul(bis_t[:, :], bis_d[:, :], bis_t[:, :]) if em.on("v") else None
                em.inc("v", ins, SM["s_dve"])
                ins = v.tensor_add(bis_hi[:, :], bis_hi[:, :], bis_t[:, :]) if em.on("v") else None
                em.inc("v", ins, SM["s_dve"])

            def bisect_end(p):
                em.wait("v", SM["s_gm"])
                gv = 0
                for hf in range(4):
                    ins = v.tensor_scalar(bis_scrap[:, 0:1024], bis_xf[:, hf * 1024:(hf + 1) * 1024],
                                          bis_lo[:, 0:1], 1.0,
                                          ALU.is_gt, ALU.mult) if em.on("v") else None
                    mk = em.inc("v", ins, SM["s_dve"])
                    em.wait("g", SM["s_dve"], mk)
                    ins = g.dma_start(out=mask_d.ap()[2 * p:2 * p + 1, hf * 1024:(hf + 1) * 1024],
                                      in_=bis_scrap[0:1, 0:1024]) if em.on("g") else None
                    em.inc("g", ins, SM["s_gm"], 16)
                    ins = g.dma_start(out=mask_d.ap()[2 * p + 1:2 * p + 2, hf * 1024:(hf + 1) * 1024],
                                      in_=bis_scrap[64:65, 0:1024]) if em.on("g") else None
                    gv = em.inc("g", ins, SM["s_gm"], 16)
                    em.wait("v", SM["s_gm"], gv)
                em.wait("q", SM["s_gm"], gv)
                lv = 0
                for hh in range(2):
                    ghd = 2 * p + hh
                    hd = ghd % 4
                    src = mask_d.ap()[ghd:ghd + 1, :].rearrange("one (c p) -> (one c) p", p=128)
                    ins = q.dma_start(out=m32_s[32 * hd:32 * hd + 32, :], in_=src) if em.on("q") else None
                    lv = em.inc("q", ins, SM["s_x"], 16)
                em.wait("t", SM["s_x"], lv)
                em.wait("t", SM["s_act"], state.get("p_trb_free", 0))
                ins = t.transpose(p_trb[:, :], m32_s, identb_s[:, :]) if em.on("t") else None
                tv2 = em.inc("t", ins, SM["s_mm"])
                em.wait("s", SM["s_mm"], tv2)
                ins = s.copy(maskT_s, p_trb[:, :]) if em.on("s") else None
                mt = em.inc("s", ins, SM["s_act"])
                state[("maskT", p)] = mt
                state["p_trb_free"] = mt

            bis_state = {}

            def bisect_step(p, budget):
                st_ = bis_state.setdefault(p, {"phase": 0, "it": 0})
                if st_["phase"] == 0:
                    bisect_begin(p)
                    st_["phase"] = 1
                while budget > 0 and st_["phase"] == 1:
                    bisect_iter()
                    st_["it"] += 1
                    budget -= 1
                    if st_["it"] >= BISECT_ITERS:
                        st_["phase"] = 2
                        bisect_end(p)

            def bisect_finish(p):
                st_ = bis_state.setdefault(p, {"phase": 0, "it": 0})
                if st_["phase"] == 0:
                    bisect_begin(p)
                    st_["phase"] = 1
                while st_["phase"] == 1:
                    bisect_iter()
                    st_["it"] += 1
                    if st_["it"] >= BISECT_ITERS:
                        st_["phase"] = 2
                        bisect_end(p)

            av_free = {0: 0, 1: 0}
            z_free = {0: 0, 1: 0}

            def pass_c(ghead, interleave=None):
                hd = ghead % 4
                em.wait("t", SM["s_act"], state[("maskT", ghead // 2)])
                em.wait("t", SM["s_dve"])
                for qc in range(8):
                    par = qc % 2
                    nkt = 4 * qc + 4
                    em.wait("t", SM["s_dve"], av_free[par])
                    em.wait("t", SM["s_dve"], z_free[par])
                    for kt in range(nkt):
                        bank = kt % 2
                        em.wait("t", SM["s_act"], state.get(("pcsc", bank), 0))
                        ins = t.matmul(ps[bank][:, 0:512], kT4[hd][:, kt * 128:(kt + 1) * 128],
                                       qT4[hd][:, qc * 512:(qc + 1) * 512], start=True, stop=True) if em.on("t") else None
                        smm = em.inc("t", ins, SM["s_mm"])
                        em.wait("s", SM["s_mm"], smm)
                        em.wait("s", SM["s_mm"], state.get(("pcPfree", bank), 0))
                        ins = s.activation(pcP_s[bank], ps[bank][:, 0:512], AF.Exp, scale=RSQ) if em.on("s") else None
                        sa = em.inc("s", ins, SM["s_act"])
                        state[("pcsc", bank)] = sa
                        em.wait("v", SM["s_act"], sa)
                        col = maskT_s[:, 32 * hd + kt:32 * hd + kt + 1]
                        if 128 * kt + 127 + 409 < 512 * qc:
                            ins = v.tensor_scalar(pcP_s[bank], pcP_s[bank], col, 0.0,
                                                  ALU.mult, ALU.add) if em.on("v") else None
                            dv2 = em.inc("v", ins, SM["s_dve"])
                        else:
                            ins = v.tensor_scalar(pcS_s[:, :], iota128c_s[:, :],
                                                  float(128 * kt - 512 * qc), 0.0,
                                                  ALU.add, ALU.add) if em.on("v") else None
                            em.inc("v", ins, SM["s_dve"])
                            ins = v.tensor_scalar(pcS410_s[:, :], pcS_s[:, :], float(WIN), 0.0,
                                                  ALU.add, ALU.add) if em.on("v") else None
                            em.inc("v", ins, SM["s_dve"])
                            ins = v.scalar_tensor_tensor(pcE_s[:, :], col, BIG, pcS410_s[:, :],
                                                         ALU.mult, ALU.add) if em.on("v") else None
                            em.inc("v", ins, SM["s_dve"])
                            ins = v.scalar_tensor_tensor(pcP_s[bank], iota512_s,
                                                         pcS_s[:, 0:1], pcP_s[bank],
                                                         ALU.is_ge, ALU.mult) if em.on("v") else None
                            em.inc("v", ins, SM["s_dve"])
                            ins = v.scalar_tensor_tensor(pcP_s[bank], iota512_s,
                                                         pcE_s[:, 0:1], pcP_s[bank],
                                                         ALU.is_lt, ALU.mult) if em.on("v") else None
                            dv2 = em.inc("v", ins, SM["s_dve"])
                        em.wait("t", SM["s_dve"], dv2)
                        ins = t.matmul(ps[4 + par][0:1, 0:512], onescb_s[:, 0:1], pcP_s[bank],
                                       start=(kt == 0), stop=(kt == nkt - 1)) if em.on("t") else None
                        em.inc("t", ins, SM["s_mm"])
                        ins = t.matmul(ps[2 + par][:, 0:512], vN4[hd][:, kt * 128:(kt + 1) * 128],
                                       pcP_s[bank], start=(kt == 0), stop=(kt == nkt - 1)) if em.on("t") else None
                        avm = em.inc("t", ins, SM["s_mm"])
                        state[("pcPfree", bank)] = avm
                    em.wait("v", SM["s_mm"])
                    ins = v.reciprocal(pcr_s[0:1, :], ps[4 + par][0:1, 0:512]) if em.on("v") else None
                    zf = em.inc("v", ins, SM["s_dve"])
                    z_free[par] = zf
                    em.wait("t", SM["s_dve"], zf)
                    em.wait("t", SM["s_act"], state.get(("pcsc", 0), 0))
                    ins = t.matmul(ps[0][:, 0:512], onesrF_s[:, :], pcr_s[0:1, :],
                                   start=True, stop=True) if em.on("t") else None
                    bm = em.inc("t", ins, SM["s_mm"])
                    em.wait("s", SM["s_mm"], bm)
                    ins = s.copy(pcrB_s, ps[0][:, 0:512]) if em.on("s") else None
                    bca = em.inc("s", ins, SM["s_act"])
                    state[("pcsc", 0)] = bca
                    em.wait("v", SM["s_act"], bca)
                    em.wait("v", SM["s_gm"])
                    ins = v.tensor_mul(pcout_s[par], ps[2 + par][:, 0:512], pcrB_s) if em.on("v") else None
                    av_free[par] = em.inc("v", ins, SM["s_dve"])
                    em.wait("g", SM["s_dve"], av_free[par])
                    r0 = ghead * 128
                    c0 = qc * 512
                    gbr(lambda r0=r0, c0=c0, par=par: g.dma_start(
                            out=sh_atA.ap()[r0:r0 + 128, c0:c0 + 512], in_=pcout_s[par]),
                        lambda r0=r0, c0=c0, par=par: g.dma_start(
                            out=sh_atB.ap()[r0:r0 + 128, c0:c0 + 512], in_=pcout_s[par]),
                        SM["s_gm"])
                    if interleave is not None:
                        interleave(qc)
                    emit_decode(40)

            for g2 in range(2):
                if STG < 2 + g2 * 3:
                    break
                proj_pair(2 * g2)
                if DBG == "qkv" and g2 == 0:
                    em.wait("g", SM["s_dve"])
                    em.wait("g", SM["s_act"])
                    em.wait("g", SM["s_mm"])
                    for i, src in enumerate([qT4[0], qT4[1], kT4[0], kT4[1]]):
                        ins = g.dma_start(out=dbgB_o[i * 128:(i + 1) * 128, :], in_=src) if em.on("g") else None
                        em.inc("g", ins, SM["s_gm"], 16)
                proj_pair(2 * g2 + 1)
                if STG < 3 + g2 * 3:
                    break
                pass_a(4 * g2 + 0)
                pass_a(4 * g2 + 1)
                bisect_step(2 * g2, 10)
                pass_a(4 * g2 + 2)
                bisect_finish(2 * g2)
                pass_a(4 * g2 + 3)
                if STG < 4 + g2 * 3:
                    break
                pass_c(4 * g2 + 0, interleave=lambda qc, p=2 * g2 + 1: bisect_step(p, 3))
                bisect_finish(2 * g2 + 1)
                pass_c(4 * g2 + 1)
                pass_c(4 * g2 + 2)
                pass_c(4 * g2 + 3)

            if STG >= 8:
                em.wait("g", SM["s_gm"])
                barrier(0)
                emit_decode(10 ** 9)
                em.wait("g", SM["s_gm"])
                barrier(1)

                def layernorm(src_f32, dst_bf, dst_cols_per_ot, t_off):
                    ins = None
                    for ot in range(NHC):
                        sl = src_f32[:, ot * 512:(ot + 1) * 512]
                        ins = v.tensor_copy(lnb_s, sl) if em.on("v") else None
                        em.inc("v", ins, SM["s_dve"])
                        em.wait("s", SM["s_dve"])
                        ins = s.activation(ln2b_s, sl, AF.Square) if em.on("s") else None
                        sa = em.inc("s", ins, SM["s_act"])
                        dvv = SM["s_dve"].n
                        em.wait("t", SM["s_dve"], dvv)
                        em.wait("t", SM["s_act"], sa)
                        ins = t.matmul(ps[4][0:1, 0:512], onescb_s[:, 0:1], lnb_s,
                                       start=(ot == 0), stop=(ot == NHC - 1)) if em.on("t") else None
                        em.inc("t", ins, SM["s_mm"])
                        ins = t.matmul(ps[5][0:1, 0:512], onescb_s[:, 0:1], ln2b_s,
                                       start=(ot == 0), stop=(ot == NHC - 1)) if em.on("t") else None
                        mmv = em.inc("t", ins, SM["s_mm"])
                        em.wait("v", SM["s_mm"], mmv)
                        em.wait("s", SM["s_mm"], mmv)
                    em.wait("v", SM["s_mm"])
                    ins = v.tensor_scalar(mean_s[0:1, :], ps[4][0:1, 0:512], 1.0 / HID, 0.0,
                                          ALU.mult, ALU.add) if em.on("v") else None
                    em.inc("v", ins, SM["s_dve"])
                    ins = v.tensor_scalar(ex2_s[0:1, :], ps[5][0:1, 0:512], 1.0 / HID, 0.0,
                                          ALU.mult, ALU.add) if em.on("v") else None
                    em.inc("v", ins, SM["s_dve"])
                    ins = v.tensor_mul(var_s[0:1, :], mean_s[0:1, :], mean_s[0:1, :]) if em.on("v") else None
                    em.inc("v", ins, SM["s_dve"])
                    ins = v.tensor_sub(var_s[0:1, :], ex2_s[0:1, :], var_s[0:1, :]) if em.on("v") else None
                    em.inc("v", ins, SM["s_dve"])
                    ins = v.tensor_scalar(var_s[0:1, :], var_s[0:1, :], float(EPS), 0.0,
                                          ALU.add, ALU.add) if em.on("v") else None
                    dvv = em.inc("v", ins, SM["s_dve"])
                    em.wait("s", SM["s_dve"], dvv)
                    ins = s.activation(sq_s[0:1, :], var_s[0:1, :], AF.Sqrt) if em.on("s") else None
                    sa = em.inc("s", ins, SM["s_act"])
                    em.wait("v", SM["s_act"], sa)
                    ins = v.reciprocal(rstd_s[0:1, :], sq_s[0:1, :]) if em.on("v") else None
                    dvv = em.inc("v", ins, SM["s_dve"])
                    em.wait("t", SM["s_dve"], dvv)
                    ins = t.matmul(ps[4][:, 0:512], onesrF_s[:, :], mean_s[0:1, :], start=True, stop=True) if em.on("t") else None
                    em.inc("t", ins, SM["s_mm"])
                    ins = t.matmul(ps[5][:, 0:512], onesrF_s[:, :], rstd_s[0:1, :], start=True, stop=True) if em.on("t") else None
                    mmv = em.inc("t", ins, SM["s_mm"])
                    em.wait("s", SM["s_mm"], mmv)
                    ins = s.copy(meanB_s, ps[4][:, 0:512]) if em.on("s") else None
                    em.inc("s", ins, SM["s_act"])
                    ins = s.copy(rstdB_s, ps[5][:, 0:512]) if em.on("s") else None
                    sa = em.inc("s", ins, SM["s_act"])
                    em.wait("v", SM["s_act"], sa)
                    for ot in range(NHC):
                        sl = src_f32[:, ot * 512:(ot + 1) * 512]
                        ins = v.tensor_sub(tmp_s, sl, meanB_s) if em.on("v") else None
                        em.inc("v", ins, SM["s_dve"])
                        if dst_bf is not None:
                            ins = v.tensor_mul(dst_bf[:, ot * dst_cols_per_ot + t_off: ot * dst_cols_per_ot + t_off + 512],
                                               tmp_s, rstdB_s) if em.on("v") else None
                            em.inc("v", ins, SM["s_dve"])
                        else:
                            ins = v.tensor_mul(sl, tmp_s, rstdB_s) if em.on("v") else None
                            dvv = em.inc("v", ins, SM["s_dve"])
                            em.wait("q", SM["s_dve"], dvv)
                            ins = q.dma_start(out=outT_o[ot * 128:(ot + 1) * 128, t_off:t_off + 512],
                                              in_=sl) if em.on("q") else None
                            em.inc("q", ins, SM["s_st"], 16)

                em.wait("t", SM["s_dve"])
                em.wait("t", SM["s_act"])
                em.wait("v", SM["s_mm"])
                em.wait("v", SM["s_act"])
                em.wait("s", SM["s_mm"])
                em.wait("s", SM["s_dve"])
                em.wait("g", SM["s_mm"])
                em.wait("q", SM["s_mm"])
                em.wait("q", SM["s_dve"])

                for tc2 in range(2):
                    em.wait("g", SM["s_mm"], state.get("atfree", 0))
                    if em.on("g"):
                        for arm in range(4):
                            c0 = arm * 1024 + tc2 * 512
                            with g.If_eq(REG["qs"], arm):
                                for hd in range(8):
                                    g.dma_start(
                                        out=at_s[:, hd * 512:(hd + 1) * 512],
                                        in_=sh_atA.ap()[hd * 128:(hd + 1) * 128, c0:c0 + 512],
                                    ).then_inc(SM["s_ga"].h, 16)
                                    g.dma_start(
                                        out=at_s[:, (8 + hd) * 512:(9 + hd) * 512],
                                        in_=sh_atB.ap()[hd * 128:(hd + 1) * 128, c0:c0 + 512],
                                    ).then_inc(SM["s_ga"].h, 16)
                    SM["s_ga"].n += 256
                    at_ld = SM["s_ga"].n
                    for og in range(4):
                        for hc in range(NHC):
                            wb = hc % 2
                            em.wait("g", SM["s_mm"], state.get(("wofree", wb), 0))
                            ins = g.dma_start(out=wo_s[wb],
                                              in_=woT[hc * 128:(hc + 1) * 128, og * 512:(og + 1) * 512]) if em.on("g") else None
                            lv = em.inc("g", ins, SM["s_w%d" % wb], 16)
                            em.wait("t", SM["s_w%d" % wb], lv)
                            em.wait("t", SM["s_ga"], at_ld)
                            ins = None
                            for oi in range(4):
                                ins = t.matmul(ps[oi][:, 0:512],
                                               wo_s[wb][:, oi * 128:oi * 128 + 128],
                                               at_s[:, hc * 512:(hc + 1) * 512],
                                               start=(hc == 0), stop=(hc == NHC - 1)) if em.on("t") else None
                            state[("wofree", wb)] = em.inc("t", ins, SM["s_mm"])
                        womm = SM["s_mm"].n
                        for oi in range(4):
                            ot = og * 4 + oi
                            xb2 = oi % 2
                            ins = q.dma_start(out=xf2[xb2],
                                              in_=xTs[ot * 128:(ot + 1) * 128, tc2 * 512:(tc2 + 1) * 512]) if em.on("q") else None
                            lv = em.inc("q", ins, SM["s_x%d" % xb2], 16)
                            em.wait("v", SM["s_x%d" % xb2], lv)
                            em.wait("v", SM["s_mm"], womm)
                            em.wait("v", SM["s_act"])
                            ins = v.tensor_add(acc_s[:, ot * 512:(ot + 1) * 512], ps[oi][:, 0:512],
                                               xf2[xb2]) if em.on("v") else None
                            dvv = em.inc("v", ins, SM["s_dve"])
                            em.wait("q", SM["s_dve"], dvv)
                            em.wait("t", SM["s_dve"], dvv)
                    layernorm(acc_s, xfb_s, 1024, tc2 * 512)
                    state["atfree"] = SM["s_mm"].n
                    em.wait("g", SM["s_dve"])

                xfb_ready = SM["s_dve"].n
                em.wait("q", SM["s_bar"], 2)
                em.wait("q", SM["s_mm"])
                for j in range(NIT):
                    wb = j % 2
                    em.wait("q", SM["s_mm"], state.get(("gufree", wb), 0))
                    src3 = sh_wg.ap()[:, j * 128:(j + 1) * 128].rearrange("(hc p) i -> p hc i", p=128)
                    ins = q.dma_start(out=gu_s[wb][:, 0:2048].rearrange("p (hc i) -> p hc i", i=128),
                                      in_=src3) if em.on("q") else None
                    em.inc("q", ins, SM["s_f%d" % wb], 16)
                    src3 = sh_wu.ap()[:, j * 128:(j + 1) * 128].rearrange("(hc p) i -> p hc i", p=128)
                    ins = q.dma_start(out=gu_s[wb][:, 2048:4096].rearrange("p (hc i) -> p hc i", i=128),
                                      in_=src3) if em.on("q") else None
                    lv = em.inc("q", ins, SM["s_f%d" % wb], 16)
                    em.wait("t", SM["s_f%d" % wb], lv)
                    em.wait("t", SM["s_dve"], xfb_ready)
                    for th in range(2):
                        em.wait("t", SM["s_dve"], state.get(("hfree",), 0))
                        ins = None
                        for hc in range(NHC):
                            ins = t.matmul(ps[0][:, 0:512], gu_s[wb][:, hc * 128: hc * 128 + 128],
                                           xfb_s[:, hc * 1024 + th * 512: hc * 1024 + th * 512 + 512],
                                           start=(hc == 0), stop=(hc == NHC - 1)) if em.on("t") else None
                        em.inc("t", ins, SM["s_mm"])
                        ins = None
                        for hc in range(NHC):
                            ins = t.matmul(ps[1][:, 0:512], gu_s[wb][:, 2048 + hc * 128: 2048 + hc * 128 + 128],
                                           xfb_s[:, hc * 1024 + th * 512: hc * 1024 + th * 512 + 512],
                                           start=(hc == 0), stop=(hc == NHC - 1)) if em.on("t") else None
                        gumm = em.inc("t", ins, SM["s_mm"])
                        state[("gufree", wb)] = gumm
                        em.wait("s", SM["s_mm"], gumm)
                        ins = s.activation(sg_s, ps[0][:, 0:512], AF.Silu) if em.on("s") else None
                        sa = em.inc("s", ins, SM["s_act"])
                        em.wait("v", SM["s_act"], sa)
                        ins = v.tensor_mul(hh_s[:, j * 1024 + th * 512: j * 1024 + th * 512 + 512],
                                           sg_s, ps[1][:, 0:512]) if em.on("v") else None
                        hdv = em.inc("v", ins, SM["s_dve"])
                        state[("hfree",)] = hdv

                em.wait("q", SM["s_mm"])
                for th in range(2):
                    for ot in range(NHC):
                        wb = ot % 2
                        em.wait("q", SM["s_mm"], state.get(("wdfree", wb), 0))
                        src3 = sh_wd.ap()[:, ot * 128:(ot + 1) * 128].rearrange("(sl p) o -> p sl o", p=128)
                        ins = q.dma_start(out=wd_s[wb][:, 0:NIT * 128].rearrange("p (sl o) -> p sl o", o=128),
                                          in_=src3) if em.on("q") else None
                        lv = em.inc("q", ins, SM["s_f%d" % wb], 16)
                        em.wait("t", SM["s_f%d" % wb], lv)
                        em.wait("t", SM["s_dve"])
                        par = ot % 2
                        em.wait("t", SM["s_dve"], state.get(("dacc", par), 0))
                        ins = None
                        for j in range(NIT):
                            ins = t.matmul(ps[2 + par][:, 0:512], wd_s[wb][:, j * 128: j * 128 + 128],
                                           hh_s[:, j * 1024 + th * 512: j * 1024 + th * 512 + 512],
                                           start=(j == 0), stop=(j == NIT - 1)) if em.on("t") else None
                        dmm = em.inc("t", ins, SM["s_mm"])
                        state[("wdfree", wb)] = dmm
                        em.wait("v", SM["s_mm"], dmm)
                        em.wait("v", SM["s_act"])
                        em.wait("v", SM["s_st"])
                        ins = v.tensor_add(acc_s[:, ot * 512:(ot + 1) * 512], ps[2 + par][:, 0:512],
                                           xfb_s[:, ot * 1024 + th * 512: ot * 1024 + th * 512 + 512]) if em.on("v") else None
                        state[("dacc", par)] = em.inc("v", ins, SM["s_dve"])
                    layernorm(acc_s, None, 0, th * 512)

                em.wait("q", SM["s_st"])
                if DBG:
                    em.wait("g", SM["s_st"])
                    em.wait("g", SM["s_dve"])
                    ins = g.dma_start(out=dbgA_o, in_=sh_atA.ap()) if em.on("g") else None
                    em.inc("g", ins, SM["s_gm"], 16)
                    ins = g.dma_start(out=dbgB_o, in_=sh_atB.ap()) if em.on("g") else None
                    em.inc("g", ins, SM["s_gm"], 16)
                    ins = g.dma_start(out=dbgX_o, in_=xfb_s) if em.on("g") else None
                    em.inc("g", ins, SM["s_gm"], 16)
            emit_decode(10 ** 9)
            if DBG in ("qkv", "rope"):
                em.wait("g", SM["s_dve"])
                em.wait("g", SM["s_act"])
                em.wait("g", SM["s_mm"])
                dump_list = [qT4[0], kT4[0], vN4[0], qT4[1], kT4[1], vN4[1],
                             AR_C[:, 0:2560]]
                for i, src in enumerate(dump_list):
                    w = src.shape[1] if hasattr(src, 'shape') else S
                    ins = g.dma_start(out=dbgA_o[i * 128:(i + 1) * 128, 0:w], in_=src) if em.on("g") else None
                    em.inc("g", ins, SM["s_gm"], 16)
            em.wait("g", SM["s_gm"])
            em.wait("g", SM["s_dk"])
            em.wait("q", SM["s_gp"])
            em.wait("q", SM["s_st"])

        @block.tensor
        def _(tt):
            schedule(Emit("t", {"t": tt, "v": None, "s": None, "g": None, "q": None}, sems))
        for sm in sems.values():
            sm.n = 0

        @block.vector
        def _(vv):
            schedule(Emit("v", {"t": None, "v": SelfSync(vv, sems["s_dve"]), "s": None, "g": None, "q": None}, sems))
        for sm in sems.values():
            sm.n = 0

        @block.scalar
        def _(ss):
            schedule(Emit("s", {"t": None, "v": None, "s": SelfSync(ss, sems["s_act"]), "g": None, "q": None}, sems))
        for sm in sems.values():
            sm.n = 0

        @block.gpsimd
        def _(gg):
            schedule(Emit("g", {"t": None, "v": None, "s": None, "g": gg, "q": None}, sems))
        for sm in sems.values():
            sm.n = 0

        @block.sync
        def _(qq):
            schedule(Emit("q", {"t": None, "v": None, "s": None, "g": None, "q": qq}, sems))

    return nc



def _dec_jobs_gu(idxT_half):
    out = np.empty((64 * 128, DECW), np.int32)
    for hc in range(16):
        for ci in range(4):
            r = (hc * 4 + ci) * 128
            out[r:r + 128, :] = idxT_half[hc * 128:(hc + 1) * 128, ci * DECW:(ci + 1) * DECW]
    return out


def _dec_jobs_d(didxT_half):
    out = np.zeros((66 * 128, DECW), np.int32)
    for rt in range(22):
        for ci, (c0, cols) in enumerate([(0, DECW), (DECW, DECW), (2 * DECW, HID - 2 * DECW)]):
            r = (rt * 3 + ci) * 128
            out[r:r + 128, 0:cols] = didxT_half[rt * 128:(rt + 1) * 128, c0:c0 + cols]
    return out

_cache = {}
_launch_counter = [0]


def kernel(L0_attn_hidden_states, L0_attn_Wq, L0_attn_Wk, L0_attn_Wv, L0_attn_Wo,
           L0_ffn_lut, L0_ffn_gate_idx, L0_ffn_up_idx, L0_ffn_down_idx):
    x = np.asarray(L0_attn_hidden_states)
    Wq = np.asarray(L0_attn_Wq)
    Wk = np.asarray(L0_attn_Wk)
    Wv = np.asarray(L0_attn_Wv)
    Wo = np.asarray(L0_attn_Wo)
    lut = np.asarray(L0_ffn_lut).astype(np.float32, copy=False)
    gidx = np.asarray(L0_ffn_gate_idx)
    uidx = np.asarray(L0_ffn_up_idx)
    didx = np.asarray(L0_ffn_down_idx)

    consts = build_consts()
    ac = np.ascontiguousarray

    xT_b = [ac(x[b].T) for b in range(2)]
    xIm_b = [ac(x[b].reshape(8, 512, 16, 128).transpose(3, 0, 2, 1).reshape(128, 65536))
             for b in range(2)]

    def wimage(W, sslot):
        return ac(W[1024 * sslot:1024 * sslot + 1024, :]
                  .reshape(4, 256, 16, 128).transpose(3, 0, 2, 1).reshape(128, 16384))

    gidxT = ac(gidx.T)
    uidxT = ac(uidx.T)
    didxT = ac(didx.T)
    woT = ac(Wo.T)

    if "fused" not in _cache:
        _cache["fused"] = build_fused()
    nc = _cache["fused"]

    _launch_counter[0] += 1
    n0 = ((int(time.time() * 1e6) ^ (_launch_counter[0] << 20)) & 0x7FFFFFFF) | 1
    n1 = ((int(time.time() * 1e3) ^ 0x5A5A5A5A ^ (_launch_counter[0] << 8)) & 0x7FFFFFFF) | 1
    nonce = np.array([[n0, n1]], np.int32)

    in_maps = []
    for c in range(8):
        sslot, gq, b = c % 2, c % 4, c // 4
        m = dict(xIm=xIm_b[b],
                 wqT=wimage(Wq, sslot),
                 wkT=wimage(Wk, sslot),
                 wvT=wimage(Wv, sslot),
                 woT=woT,
                 xTs=ac(xT_b[b][:, gq * 1024:gq * 1024 + 1024]),
                 gidxT=_dec_jobs_gu(gidxT[:, 0:DHALF] if sslot == 0 else gidxT[:, 2688:INTER]),
                 uidxT=_dec_jobs_gu(uidxT[:, 0:DHALF] if sslot == 0 else uidxT[:, 2688:INTER]),
                 didxT=_dec_jobs_d(didxT[0:DHALF, :] if sslot == 0 else didxT[2688:INTER, :]),
                 lut16=lut.reshape(1, 16),
                 myslot=np.array([[sslot]], np.int32),
                 qsel=np.array([[gq]], np.int32),
                 nonce=nonce)
        for k in CONST_KEYS:
            m[k] = consts[k]
        in_maps.append(m)
    res = run_bass_kernel_spmd(nc, in_maps, core_ids=list(range(8)))

    out = np.empty((2, S, HID), dtype=np.float32)
    for c in range(8):
        gq, b = c % 4, c // 4
        t0 = gq * 1024
        out[b, t0:t0 + 1024, :] = np.asarray(res.results[c]["outT"]).T
    return out


# version 07
# speedup vs baseline: 1.0064x; 1.0064x over previous
import time
import numpy as np
import concourse.bass as bass
import concourse.mybir as mybir
from concourse.bass_utils import run_bass_kernel_spmd
from contextlib import ExitStack

F32 = mybir.dt.float32
BF16 = mybir.dt.bfloat16
I32 = mybir.dt.int32
AF = mybir.ActivationFunctionType
ALU = mybir.AluOpType

H = 16
D = 128
HID = 2048
S = 4096
INTER = 5504
NIT = 43
NHC = 16
WIN = 410
RSQ = float(1.0 / np.sqrt(128.0))
BIG = 1.0e9
BISECT_ITERS = 42
HEAVY_K = 409.0
EPS = 1.0e-5
DECW = 704
DHALF = 2816
POLL_LIMIT = 200_000_000


class TSem:
    def __init__(self, h):
        self.h = h
        self.n = 0


class Emit:
    def __init__(self, eng, handles, sems):
        self.eng = eng
        self.h = handles
        self.sems = sems
        self._waited = {}

    def on(self, eng):
        return self.eng == eng

    def wait(self, eng, sem, val=None):
        if val is None:
            val = sem.n
        if val <= 0:
            return
        if self.eng == eng:
            key = id(sem)
            if self._waited.get(key, 0) < val:
                self.h[eng].wait_ge(sem.h, val)
                self._waited[key] = val

    def inc(self, eng, ins, sem, k=1):
        if self.eng == eng and ins is not None:
            ins.then_inc(sem.h, k)
        sem.n += k
        return sem.n


_NON_EMIT = {"wait_ge", "then_inc", "sem_clear", "engine_nop"}


class SelfSync:

    def __init__(self, h, tsem):
        self._h = h
        self._tsem = tsem

    def __getattr__(self, name):
        fn = getattr(self._h, name)
        if name.startswith("_") or name in _NON_EMIT or not callable(fn):
            return fn
        tsem = self._tsem
        h = self._h

        def wrapped(*a, **k):
            if tsem.n > 0:
                h.wait_ge(tsem.h, tsem.n)
            return fn(*a, **k)
        return wrapped


def build_consts():
    import ml_dtypes
    c = {}
    inv_freq = (1.0 / (10000.0 ** (np.arange(0, D, 2, dtype=np.float32) / np.float32(D)))).astype(np.float32)
    pos = np.arange(S, dtype=np.float32)
    freqs = pos[:, None] * inv_freq[None, :]
    emb = np.concatenate([freqs, freqs], axis=-1)
    c["cosbC"] = np.ascontiguousarray(np.cos(emb).astype(np.float32).T).astype(ml_dtypes.bfloat16)
    c["sinbC"] = np.ascontiguousarray(np.sin(emb).astype(np.float32).T).astype(ml_dtypes.bfloat16)
    R = np.zeros((D, D), dtype=np.float32)
    for i in range(D):
        if i < 64:
            R[i + 64, i] = -1.0
        else:
            R[i - 64, i] = 1.0
    c["rotC"] = R.astype(ml_dtypes.bfloat16)
    c["identC"] = np.eye(128, dtype=np.float32).astype(ml_dtypes.bfloat16)
    qi = np.arange(128)[:, None]
    ki = np.arange(128)[None, :]
    c["triC"] = (ki <= qi).astype(np.float32).astype(ml_dtypes.bfloat16)
    c["iota512C"] = np.broadcast_to(np.arange(512, dtype=np.float32)[None, :], (128, 512)).copy()
    c["iota128cC"] = np.arange(128, dtype=np.float32)[:, None].copy()
    c["onesrC"] = np.ones((1, 128), dtype=np.float32)
    return c


CONST_KEYS = ["cosbC", "sinbC", "rotC", "identC", "triC", "iota512C", "iota128cC", "onesrC"]

B_XB = 0
B_PA0 = 4096
B_PA1 = 4096
B_WQ = 8192
B_WK = 12288
B_WV = 16384
B_BIS = 20480
B_DEC = 21504
B_PCP = 24320
B_PCO = 25344
B_QSB = 25856
B_M32 = 26368
B_COS = 26496
B_SIN = 27008
B_END = 27712
B_XFB = 0
B_AT = 16384
B_WO = 24576
B_GU = 16384
B_WD = 16384

C_ACC = 0
C_XF2 = 8192
C_MEANB = 9216
C_RSTDB = 9728
C_TMP = 10240
C_MASKT = 10752
C_ZPART = 10880
C_LUTB = 10896
C_END = 11008

A_HH = 0
A_LNB = 44032
A_LN2B = 44544
A_SG = 45056
A_END = 49152


def build_fused():
    nc = bass.Bass("TRN2", target_bir_lowering=False, debug=False, dynamic_dma_scratch_size=16384)
    dt = nc.dram_tensor
    xIm = dt("xIm", [128, 8 * NHC * 512], F32, kind="ExternalInput").ap()
    wqT = dt("wqT", [128, 4 * 4096], F32, kind="ExternalInput").ap()
    wkT = dt("wkT", [128, 4 * 4096], F32, kind="ExternalInput").ap()
    wvT = dt("wvT", [128, 4 * 4096], F32, kind="ExternalInput").ap()
    woT = dt("woT", [HID, HID], F32, kind="ExternalInput").ap()
    xTs = dt("xTs", [HID, 1024], F32, kind="ExternalInput").ap()
    gidxT = dt("gidxT", [64 * 128, DECW], I32, kind="ExternalInput").ap()
    uidxT = dt("uidxT", [64 * 128, DECW], I32, kind="ExternalInput").ap()
    didxT = dt("didxT", [66 * 128, DECW], I32, kind="ExternalInput").ap()
    lut16 = dt("lut16", [1, 16], F32, kind="ExternalInput").ap()
    myslot = dt("myslot", [1, 1], I32, kind="ExternalInput").ap()
    qsel = dt("qsel", [1, 1], I32, kind="ExternalInput").ap()
    nonce = dt("nonce", [1, 2], I32, kind="ExternalInput").ap()
    cosb_d = dt("cosbC", [D, S], BF16, kind="ExternalInput").ap()
    sinb_d = dt("sinbC", [D, S], BF16, kind="ExternalInput").ap()
    rot_d = dt("rotC", [D, D], BF16, kind="ExternalInput").ap()
    ident_d = dt("identC", [128, 128], BF16, kind="ExternalInput").ap()
    tri_d = dt("triC", [128, 128], BF16, kind="ExternalInput").ap()
    iota512_d = dt("iota512C", [128, 512], F32, kind="ExternalInput").ap()
    iota128c_d = dt("iota128cC", [128, 1], F32, kind="ExternalInput").ap()
    onesr_d = dt("onesrC", [1, 128], F32, kind="ExternalInput").ap()

    outT_o = dt("outT", [HID, 1024], F32, kind="ExternalOutput").ap()
    import os as _osd
    DBG = _osd.environ.get("FUSED_DEBUG", "")
    if DBG:
        dbgA_o = dt("dbgA", [1024, S], BF16, kind="ExternalOutput").ap()
        dbgB_o = dt("dbgB", [1024, S], BF16, kind="ExternalOutput").ap()
        dbgX_o = dt("dbgX", [128, 16384], BF16, kind="ExternalOutput").ap()

    tsum_d = dt("tsum_d", [8, S], F32)
    mask_d = dt("mask_d", [8, S], BF16)
    sh_atA = dt("sh_atA", [1024, S], BF16, addr_space="Shared")
    sh_atB = dt("sh_atB", [1024, S], BF16, addr_space="Shared")
    sh_wg = dt("sh_wg", [HID, INTER], BF16, addr_space="Shared")
    sh_wu = dt("sh_wu", [HID, INTER], BF16, addr_space="Shared")
    sh_wd = dt("sh_wd", [INTER, HID], BF16, addr_space="Shared")
    shf = dt("shf", [2, 16], I32, addr_space="Shared")

    import os
    STG = int(os.environ.get("FUSED_STAGES", "9"))
    PLIMIT = int(os.environ.get("FUSED_POLL_LIMIT", str(POLL_LIMIT)))

    with ExitStack() as ctx:
        e = ctx.enter_context
        AR_A = e(nc.sbuf_tensor("AR_A", [128, A_END], BF16))
        AR_B = e(nc.sbuf_tensor("AR_B", [128, B_END], BF16))
        AR_C = e(nc.sbuf_tensor("AR_C", [128, C_END], F32))
        rot_s = e(nc.sbuf_tensor("rot_s", [D, D], BF16))
        identb_s = e(nc.sbuf_tensor("identb_s", [128, 128], BF16))
        tri_s = e(nc.sbuf_tensor("tri_s", [128, 128], BF16))
        onesrF_s = e(nc.sbuf_tensor("onesrF_s", [1, 128], F32))
        onescb_s = e(nc.sbuf_tensor("onescb_s", [128, 1], BF16))
        lutrow_s = e(nc.sbuf_tensor("lutrow_s", [1, 16], F32))
        iota128c_s = e(nc.sbuf_tensor("iota128c_s", [128, 1], F32))
        zsum_s = e(nc.sbuf_tensor("zsum_s", [128, 1], F32))
        rinv_s = e(nc.sbuf_tensor("rinv_s", [128, 1], F32))
        rinvb_s = e(nc.sbuf_tensor("rinvb_s", [128, 1], BF16))
        bis_lo = e(nc.sbuf_tensor("bis_lo", [128, 1], F32))
        bis_hi = e(nc.sbuf_tensor("bis_hi", [128, 1], F32))
        bis_mid = e(nc.sbuf_tensor("bis_mid", [128, 1], F32))
        bis_cnt = e(nc.sbuf_tensor("bis_cnt", [128, 1], F32))
        bis_pred = e(nc.sbuf_tensor("bis_pred", [128, 1], F32))
        bis_d = e(nc.sbuf_tensor("bis_d", [128, 1], F32))
        bis_t = e(nc.sbuf_tensor("bis_t", [128, 1], F32))
        pcS_s = e(nc.sbuf_tensor("pcS_s", [128, 1], F32))
        pcS410_s = e(nc.sbuf_tensor("pcS410_s", [128, 1], F32))
        pcE_s = e(nc.sbuf_tensor("pcE_s", [128, 1], F32))
        mean_s = e(nc.sbuf_tensor("mean_s", [1, 512], F32))
        ex2_s = e(nc.sbuf_tensor("ex2_s", [1, 512], F32))
        var_s = e(nc.sbuf_tensor("var_s", [1, 512], F32))
        sq_s = e(nc.sbuf_tensor("sq_s", [1, 512], F32))
        rstd_s = e(nc.sbuf_tensor("rstd_s", [1, 512], F32))
        pcr_s = e(nc.sbuf_tensor("pcr_s", [1, 512], F32))
        sslot_s = e(nc.sbuf_tensor("sslot_s", [1, 1], I32))
        sqsel_s = e(nc.sbuf_tensor("sqsel_s", [1, 1], I32))
        snon_s = e(nc.sbuf_tensor("snon_s", [1, 2], I32))

        ps = [e(nc.psum_tensor(f"ps{i}", [128, 512], F32)) for i in range(7)]
        p_trb = e(nc.psum_tensor("p_trb", [128, 128], BF16))

        sems = {n: TSem(e(nc.semaphore(n)))
                for n in ["s_ld", "s_x", "s_mm", "s_act", "s_dve", "s_st",
                          "s_gp", "s_dk", "s_gm", "s_ga", "s_w0", "s_w1",
                          "s_xb", "s_wt", "s_fl", "s_x0", "s_x1", "s_f0",
                          "s_f1", "s_bar"]}
        block = e(nc.Block())

        qT4 = [AR_A[:, (4 + i) * 4096:(5 + i) * 4096] for i in range(4)]
        kT4 = [AR_A[:, i * 4096:(i + 1) * 4096] for i in range(4)]
        vN4 = [AR_A[:, (8 + i) * 4096:(9 + i) * 4096] for i in range(4)]
        hh_s = AR_A[:, A_HH:A_HH + NIT * 1024]
        lnb_s = AR_A[:, A_LNB:A_LNB + 512]
        ln2b_s = AR_A[:, A_LN2B:A_LN2B + 512]
        sg_s = AR_A[:, A_SG:A_SG + 512]

        xb_s = AR_B[:, B_XB:B_XB + 4096]
        pA_s = [AR_B[:, B_PA0:B_PA0 + 4096], AR_B[:, B_PA0:B_PA0 + 4096]]
        wq_s = AR_B[:, B_WQ:B_WQ + 4096]
        wk_s = AR_B[:, B_WK:B_WK + 4096]
        wv_s = AR_B[:, B_WV:B_WV + 4096]
        bis_scrap = AR_B[:, B_BIS:B_BIS + 1024]
        dec_idx = [AR_B[:, B_DEC + i * DECW:B_DEC + (i + 1) * DECW] for i in range(2)]
        dec_acc = [AR_B[:, B_DEC + 2 * DECW:B_DEC + 3 * DECW] for i in range(2)]
        dec_tmp = AR_B[:, B_DEC + 3 * DECW:B_DEC + 4 * DECW]
        pcP_s = [AR_B[:, B_PCP + i * 512:B_PCP + (i + 1) * 512] for i in range(2)]
        pcout_s = [AR_B[:, B_PCO:B_PCO + 512] for i in range(2)]
        qsb_s = AR_B[:, B_QSB:B_QSB + 512]
        m32_s = AR_B[:, B_M32:B_M32 + 128]
        cos_s = AR_B[:, B_COS:B_COS + 512]
        sin_s = AR_B[:, B_SIN:B_SIN + 512]
        rotb_s = AR_A[:, A_SG:A_SG + 512]
        xfb_s = AR_B[:, B_XFB:B_XFB + 16384]
        at_s = AR_B[:, B_AT:B_AT + 8192]
        wo_s = [AR_B[:, B_WO + i * 512:B_WO + (i + 1) * 512] for i in range(2)]
        gu_s = [AR_B[:, B_GU + i * 4096:B_GU + (i + 1) * 4096] for i in range(2)]
        wd_s = [AR_B[:, B_WD + i * 5632:B_WD + (i + 1) * 5632] for i in range(2)]

        acc_s = AR_C[:, C_ACC:C_ACC + 8192]
        bis_xf = AR_C[:, C_ACC:C_ACC + 4096]
        vpart_s = AR_C[:, C_ACC + 4096:C_ACC + 5120]
        xf2 = [AR_C[:, C_XF2 + i * 512:C_XF2 + (i + 1) * 512] for i in range(2)]
        rope_t1 = AR_C[:, C_XF2:C_XF2 + 512]
        rope_t2 = AR_C[:, C_XF2 + 512:C_XF2 + 1024]
        meanB_s = AR_C[:, C_MEANB:C_MEANB + 512]
        tstage_s = AR_C[:, C_MEANB:C_MEANB + 512]
        rstdB_s = AR_C[:, C_RSTDB:C_RSTDB + 512]
        pcrB_s = AR_C[:, C_RSTDB:C_RSTDB + 512]
        tmp_s = AR_C[:, C_TMP:C_TMP + 512]
        iota512_s = AR_C[:, C_TMP:C_TMP + 512]
        maskT_s = AR_C[:, C_MASKT:C_MASKT + 128]
        zpart_s = AR_C[:, C_ZPART:C_ZPART + 16]
        lutb_s = AR_C[:, C_LUTB:C_LUTB + 16]

        state = {}
        REG = {}

        def schedule(em: Emit):
            state.clear()
            REG.clear()
            SM = sems
            t = em.h.get("t")
            v = em.h.get("v")
            s = em.h.get("s")
            g = em.h.get("g")
            q = em.h.get("q")

            for dst, src in [(iota128c_s[:, :], iota128c_d), (lutrow_s[0:1, :], lut16),
                             (onesrF_s[0:1, :], onesr_d), (sslot_s[0:1, :], myslot),
                             (sqsel_s[0:1, :], qsel), (snon_s[0:1, :], nonce)]:
                ins = q.dma_start(out=dst, in_=src) if em.on("q") else None
                em.inc("q", ins, SM["s_ld"], 16)
            ins = q.dma_start(out=iota512_s, in_=iota512_d) if em.on("q") else None
            em.inc("q", ins, SM["s_ld"], 16)
            for dst, srcc in [(rot_s[:, :], rot_d), (identb_s[:, :], ident_d), (tri_s[:, :], tri_d)]:
                ins = q.dma_start(out=dst, in_=srcc) if em.on("q") else None
                em.inc("q", ins, SM["s_ld"], 16)
            lv0 = SM["s_ld"].n
            if em.on("g"):
                for rn in ["slot", "qs", "nonr", "flag", "cond", "c2", "itr"]:
                    REG[rn] = g.alloc_register(rn)
                g.wait_ge(SM["s_ld"].h, lv0)
                g.load(REG["slot"], sslot_s[0:1, 0:1])
                g.load(REG["qs"], sqsel_s[0:1, 0:1])
            em._waited[id(SM["s_ld"])] = max(em._waited.get(id(SM["s_ld"]), 0), lv0 if em.on("g") else 0)
            ins = v.memset(onescb_s[:, :], 1.0) if em.on("v") else None
            em.inc("v", ins, SM["s_dve"])
            ins = v.memset(m32_s, 0.0) if em.on("v") else None
            em.inc("v", ins, SM["s_dve"])
            ins = v.memset(bis_xf[:, :], 0.0) if em.on("v") else None
            em.inc("v", ins, SM["s_dve"])
            em.wait("t", SM["s_ld"])
            ins = t.matmul(ps[5][:, 0:16], onesrF_s[:, :], lutrow_s[0:1, :], start=True, stop=True) if em.on("t") else None
            em.inc("t", ins, SM["s_mm"])
            em.wait("s", SM["s_mm"])
            ins = s.copy(lutb_s, ps[5][:, 0:16]) if em.on("s") else None
            lut_ready = em.inc("s", ins, SM["s_act"])

            def gbr(mk0, mk1, sem, k=16):
                if em.on("g"):
                    with g.If_eq(REG["slot"], 0):
                        mk0().then_inc(sem.h, k)
                    with g.Else():
                        mk1().then_inc(sem.h, k)
                sem.n += k
                return sem.n

            dec_jobs = []
            NODEC = os.environ.get("FUSED_NODEC") == "1"
            for srcd, dstd in [] if NODEC else [(gidxT, sh_wg), (uidxT, sh_wu)]:
                for hc in range(NHC):
                    for ci, cc in enumerate(range(0, DHALF, DECW)):
                        dec_jobs.append((srcd, hc * 4 + ci, dstd, hc * 128, cc, DECW, False))
            for rt in [] if NODEC else range(22):
                for ci, (cc, cols) in enumerate([(0, DECW), (DECW, DECW), (2 * DECW, HID - 2 * DECW)]):
                    dec_jobs.append((didxT, rt * 3 + ci, sh_wd, rt * 128, cc, cols, True))
            dec_buf_free = [0, 0]

            def decode_ops():
                for j, (srcd, jrow, dstd, r0, c0, cols, is_down) in enumerate(dec_jobs):
                    buf = dec_idx[j % 2]
                    acc = dec_acc[j % 2]
                    rows = 128

                    def op_load(j=j, srcd=srcd, jrow=jrow, buf=buf):
                        em.wait("g", SM["s_gm"])
                        ins = g.dma_start(out=buf[:, 0:DECW],
                                          in_=srcd[jrow * 128:(jrow + 1) * 128, :]) if em.on("g") else None
                        lv = em.inc("g", ins, SM["s_dk"], 16)
                        state[("dk", j)] = lv
                    yield op_load

                    def op_first(j=j, buf=buf, acc=acc):
                        em.wait("v", SM["s_dk"], state[("dk", j)])
                        em.wait("v", SM["s_act"], lut_ready)
                        em.wait("v", SM["s_gm"], max(dec_buf_free[0], dec_buf_free[1]))
                        ins = v.tensor_scalar(acc[:, 0:DECW], buf[:, 0:DECW], 0.0,
                                              lutb_s[:, 0:1], ALU.is_equal, ALU.mult) if em.on("v") else None
                        em.inc("v", ins, SM["s_dve"])
                    yield op_first

                    for tv in range(1, 16):
                        def op_sel(tv=tv, buf=buf):
                            ins = v.tensor_scalar(dec_tmp[:, 0:DECW], buf[:, 0:DECW], float(tv),
                                                  lutb_s[:, tv:tv + 1], ALU.is_equal, ALU.mult) if em.on("v") else None
                            em.inc("v", ins, SM["s_dve"])
                        yield op_sel

                        def op_add(acc=acc):
                            ins = v.tensor_add(acc[:, 0:DECW], acc[:, 0:DECW],
                                               dec_tmp[:, 0:DECW]) if em.on("v") else None
                            em.inc("v", ins, SM["s_dve"])
                        yield op_add

                    def op_store(j=j, dstd=dstd, r0=r0, c0=c0, cols=cols,
                                 acc=acc, is_down=is_down):
                        em.wait("g", SM["s_dve"])
                        if is_down:
                            sv = gbr(lambda: g.dma_start(out=dstd.ap()[r0:r0 + 128, c0:c0 + cols],
                                                         in_=acc[:, 0:cols]),
                                     lambda: g.dma_start(out=dstd.ap()[2688 + r0:2688 + r0 + 128, c0:c0 + cols],
                                                         in_=acc[:, 0:cols]),
                                     SM["s_gm"])
                        else:
                            sv = gbr(lambda: g.dma_start(out=dstd.ap()[r0:r0 + 128, c0:c0 + cols],
                                                         in_=acc[:, 0:cols]),
                                     lambda: g.dma_start(out=dstd.ap()[r0:r0 + 128, 2688 + c0:2688 + c0 + cols],
                                                         in_=acc[:, 0:cols]),
                                     SM["s_gm"])
                        dec_buf_free[j % 2] = sv
                    yield op_store

            dec_stream = decode_ops()
            dec_done = [False]

            def emit_decode(n):
                for _ in range(n):
                    if dec_done[0]:
                        return
                    op = next(dec_stream, None)
                    if op is None:
                        dec_done[0] = True
                        return
                    op()

            def barrier(which):
                colf = [(0, 8), (4, 12)][which]
                gbr(lambda: g.dma_start(out=shf.ap()[0:1, colf[1]:colf[1] + 1], in_=snon_s[0:1, 1:2]),
                    lambda: g.dma_start(out=shf.ap()[1:2, colf[1]:colf[1] + 1], in_=snon_s[0:1, 1:2]),
                    SM["s_fl"])
                em.wait("g", SM["s_fl"])
                gbr(lambda: g.dma_start(out=shf.ap()[0:1, colf[0]:colf[0] + 1], in_=snon_s[0:1, 0:1]),
                    lambda: g.dma_start(out=shf.ap()[1:2, colf[0]:colf[0] + 1], in_=snon_s[0:1, 0:1]),
                    SM["s_fl"])
                em.wait("g", SM["s_fl"])
                if em.on("g"):
                    for word, col in [(0, colf[0]), (1, colf[1])]:
                        g.load(REG["nonr"], snon_s[0:1, word:word + 1])
                        g.reg_mov(REG["cond"], 1)
                        g.reg_mov(REG["itr"], 0)
                        with g.If_eq(REG["slot"], 0):
                            with g.While(REG["cond"]):
                                g.load(REG["flag"], shf.ap()[1:2, col:col + 1])
                                g.reg_alu(REG["cond"], REG["flag"], REG["nonr"], ALU.not_equal)
                                g.reg_add(REG["itr"], REG["itr"], 1)
                                g.reg_alu(REG["c2"], REG["itr"], PLIMIT, ALU.is_lt)
                                g.reg_alu(REG["cond"], REG["cond"], REG["c2"], ALU.logical_and)
                        with g.Else():
                            with g.While(REG["cond"]):
                                g.load(REG["flag"], shf.ap()[0:1, col:col + 1])
                                g.reg_alu(REG["cond"], REG["flag"], REG["nonr"], ALU.not_equal)
                                g.reg_add(REG["itr"], REG["itr"], 1)
                                g.reg_alu(REG["c2"], REG["itr"], PLIMIT, ALU.is_lt)
                                g.reg_alu(REG["cond"], REG["cond"], REG["c2"], ALU.logical_and)
                    g.sem_inc(SM["s_bar"].h, 1)
                SM["s_bar"].n += 1

            def proj_pair(gp):
                pl = gp % 2
                em.wait("t", SM["s_dve"])
                em.wait("t", SM["s_act"])
                for wsb, wsrc in [(wq_s, wqT), (wk_s, wkT), (wv_s, wvT)]:
                    em.wait("g", SM["s_mm"], state.get("wfree", 0))
                    ins = g.dma_start(out=wsb,
                                      in_=wsrc[:, gp * 4096:(gp + 1) * 4096]) if em.on("g") else None
                    em.inc("g", ins, SM["s_wt"], 16)
                wt_ready = SM["s_wt"].n
                for tc in range(8):
                    em.wait("q", SM["s_dve"], state.get("cs_free", 0))
                    ins = q.dma_start(out=cos_s, in_=cosb_d[:, tc * 512:(tc + 1) * 512]) if em.on("q") else None
                    em.inc("q", ins, SM["s_x"], 16)
                    ins = q.dma_start(out=sin_s, in_=sinb_d[:, tc * 512:(tc + 1) * 512]) if em.on("q") else None
                    cs_ld = em.inc("q", ins, SM["s_x"], 16)
                    for half in range(2):
                        for quarter in range(2):
                            hcbase = half * 8 + quarter * 4
                            em.wait("g", SM["s_mm"], state.get(("xbq", quarter), 0))
                            ins = g.dma_start(out=xb_s[:, quarter * 2048:(quarter + 1) * 2048],
                                              in_=xIm[:, tc * 8192 + hcbase * 512:
                                                      tc * 8192 + hcbase * 512 + 2048]) if em.on("g") else None
                            xlv = em.inc("g", ins, SM["s_xb"], 16)
                            em.wait("t", SM["s_xb"], xlv)
                            em.wait("t", SM["s_wt"], wt_ready)
                            for hh in range(2):
                                for wi, wsb in [(0, wq_s), (1, wk_s)]:
                                    pp = ps[wi * 2 + hh]
                                    ins = None
                                    for hq in range(4):
                                        hc = hcbase + hq
                                        ins = t.matmul(pp[:, :],
                                                       wsb[:, hc * 256 + hh * 128: hc * 256 + hh * 128 + 128],
                                                       xb_s[:, (quarter * 4 + hq) * 512:(quarter * 4 + hq) * 512 + 512],
                                                       start=(hc == 0), stop=(hc == NHC - 1)) if em.on("t") else None
                                    em.inc("t", ins, SM["s_mm"])
                        for tp in range(2):
                            bank = 4 + tp
                            em.wait("t", SM["s_act"] if half == 1 else SM["s_dve"],
                                    state.get(("vfree", bank), 0))
                            ins = None
                            for tt in (2 * tp, 2 * tp + 1):
                                cr = (tt % 2) * 256
                                for hh in range(2):
                                    for quarter in range(2):
                                        for hq in range(4):
                                            hc = half * 8 + quarter * 4 + hq
                                            ins = t.matmul(ps[bank][:, cr + hh * 128: cr + hh * 128 + 128],
                                                           xb_s[:, (quarter * 4 + hq) * 512 + tt * 128:
                                                                (quarter * 4 + hq) * 512 + tt * 128 + 128],
                                                           wv_s[:, hc * 256 + hh * 128: hc * 256 + hh * 128 + 128],
                                                           start=(quarter == 0 and hq == 0),
                                                           stop=(quarter == 1 and hq == 3)) if em.on("t") else None
                            state[("vmm", half, tp)] = em.inc("t", ins, SM["s_mm"])
                        state[("xbq", 0)] = SM["s_mm"].n
                        state[("xbq", 1)] = SM["s_mm"].n
                        if half == 0:
                            for tp in range(2):
                                bank = 4 + tp
                                em.wait("s", SM["s_mm"], state[("vmm", half, tp)])
                                em.wait("s", SM["s_dve"], state.get("vpart_free", 0))
                                ins = s.copy(vpart_s[:, tp * 512:(tp + 1) * 512],
                                             ps[bank][:, 0:512]) if em.on("s") else None
                                state[("vfree", bank)] = em.inc("s", ins, SM["s_act"])
                        else:
                            for tp in range(2):
                                bank = 4 + tp
                                em.wait("v", SM["s_mm"], state[("vmm", half, tp)])
                                em.wait("v", SM["s_act"])
                                for tt in (2 * tp, 2 * tp + 1):
                                    cr = (tt % 2) * 256
                                    for hh in range(2):
                                        slot = pl * 2 + hh
                                        ins = v.tensor_add(
                                            vN4[slot][:, (tc * 4 + tt) * 128:(tc * 4 + tt) * 128 + 128],
                                            ps[bank][:, cr + hh * 128:cr + hh * 128 + 128],
                                            vpart_s[:, tt * 256 + hh * 128:tt * 256 + hh * 128 + 128]) if em.on("v") else None
                                        state[("vfree", bank)] = em.inc("v", ins, SM["s_dve"])
                            state["vpart_free"] = SM["s_dve"].n
                    if DBG == "qkv" and gp == 1 and tc == 7:
                        em.wait("s", SM["s_mm"])
                        for bi in range(4):
                            ins = s.copy(AR_C[:, bi * 512:(bi + 1) * 512], ps[bi][:, 0:512]) if em.on("s") else None
                            em.inc("s", ins, SM["s_act"])
                    em.wait("v", SM["s_x"], cs_ld)
                    for hh in range(2):
                        slot = pl * 2 + hh
                        for wi, dstT in [(0, qT4), (1, kT4)]:
                            pp = ps[wi * 2 + hh]
                            em.wait("s", SM["s_mm"])
                            em.wait("s", SM["s_mm"], state.get("qsb_free", 0))
                            em.wait("s", SM["s_dve"], state.get("qsb_free_dve", 0))
                            ins = s.copy(qsb_s, pp[:, :]) if em.on("s") else None
                            ev = em.inc("s", ins, SM["s_act"])
                            em.wait("t", SM["s_act"], ev)
                            em.wait("t", SM["s_dve"], state.get("rot_free", 0))
                            ins = t.matmul(ps[6][:, :], rot_s[:, :], qsb_s,
                                           start=True, stop=True) if em.on("t") else None
                            rot_done = em.inc("t", ins, SM["s_mm"])
                            state["qsb_free"] = rot_done
                            em.wait("v", SM["s_mm"], rot_done)
                            ins = v.tensor_mul(rope_t1, ps[6][:, :], sin_s) if em.on("v") else None
                            em.inc("v", ins, SM["s_dve"])
                            ins = v.tensor_mul(rope_t2, pp[:, :], cos_s) if em.on("v") else None
                            em.inc("v", ins, SM["s_dve"])
                            ins = v.tensor_add(dstT[slot][:, tc * 512:(tc + 1) * 512],
                                               rope_t1, rope_t2) if em.on("v") else None
                            rope_done = em.inc("v", ins, SM["s_dve"])
                            state["rot_free"] = rope_done
                            state["qsb_free_dve"] = rope_done
                            em.wait("t", SM["s_dve"], rope_done)
                            if DBG == "rope" and gp == 0 and tc == 0 and hh == 0 and wi == 0:
                                em.wait("s", SM["s_dve"], rope_done)
                                ins = s.copy(AR_C[:, 0:512], qsb_s) if em.on("s") else None
                                em.inc("s", ins, SM["s_act"])
                                ins = s.copy(AR_C[:, 512:1024], rotb_s) if em.on("s") else None
                                em.inc("s", ins, SM["s_act"])
                                ins = v.tensor_copy(AR_C[:, 1024:1536], rope_t1) if em.on("v") else None
                                em.inc("v", ins, SM["s_dve"])
                                ins = v.tensor_copy(AR_C[:, 1536:2048], rope_t2) if em.on("v") else None
                                em.inc("v", ins, SM["s_dve"])
                                ins = s.copy(AR_C[:, 2048:2560], ps[wi * 2 + hh][:, 0:512]) if em.on("s") else None
                                em.inc("s", ins, SM["s_act"])
                    state["cs_free"] = SM["s_dve"].n
                    emit_decode(60)
                state["wfree"] = SM["s_mm"].n

            cs_banks = [2, 3, 4]
            pa_free = {0: 0, 1: 0}

            def pass_a(ghead):
                hd = ghead % 4
                em.wait("t", SM["s_dve"])
                b = None
                for qt in range(32):
                    b = qt % 2
                    kmax = (qt + 1) * 128
                    pieces = []
                    cpos = 0
                    while cpos < kmax - 128:
                        w = min(512, kmax - 128 - cpos)
                        pieces.append((cpos, w, False))
                        cpos += w
                    pieces.append((kmax - 128, 128, True))
                    em.wait("s", SM["s_mm"], max(pa_free[0], pa_free[1]))
                    npc = len(pieces)
                    for pi, (c0, w, is_diag) in enumerate(pieces):
                        bank = ps[pi % 2]
                        em.wait("t", SM["s_act"], state.get(("scfree", pi % 2), 0))
                        ins = t.matmul(bank[:, 0:w], qT4[hd][:, qt * 128:(qt + 1) * 128],
                                       kT4[hd][:, c0:c0 + w], start=True, stop=True) if em.on("t") else None
                        smm = em.inc("t", ins, SM["s_mm"])
                        em.wait("s", SM["s_mm"], smm)
                        if not is_diag:
                            ins = s.activation(pA_s[b][:, c0:c0 + w], bank[:, 0:w], AF.Exp, scale=RSQ,
                                               accum_out=zpart_s[:, pi:pi + 1]) if em.on("s") else None
                            sa = em.inc("s", ins, SM["s_act"])
                        else:
                            ins = s.activation(pA_s[b][:, c0:c0 + w], bank[:, 0:w], AF.Exp, scale=RSQ) if em.on("s") else None
                            sa = em.inc("s", ins, SM["s_act"])
                            em.wait("v", SM["s_act"], sa)
                            ins = v.scalar_tensor_tensor(pA_s[b][:, c0:c0 + w], tri_s[:, :], 1.0,
                                                         pA_s[b][:, c0:c0 + w], ALU.mult, ALU.mult,
                                                         accum_out=zpart_s[:, pi:pi + 1]) if em.on("v") else None
                            em.inc("v", ins, SM["s_dve"])
                        state[("scfree", pi % 2)] = sa
                    em.wait("v", SM["s_act"])
                    ins = v.reduce_sum(zsum_s[:, 0:1], zpart_s[:, 0:npc], axis=mybir.AxisListType.X) if em.on("v") else None
                    em.inc("v", ins, SM["s_dve"])
                    ins = v.reciprocal(rinv_s[:, 0:1], zsum_s[:, 0:1]) if em.on("v") else None
                    em.inc("v", ins, SM["s_dve"])
                    ins = v.tensor_copy(rinvb_s[:, 0:1], rinv_s[:, 0:1]) if em.on("v") else None
                    rv = em.inc("v", ins, SM["s_dve"])
                    kpad = ((kmax + 511) // 512) * 512
                    if kpad > kmax:
                        em.wait("v", SM["s_mm"], max(pa_free[0], pa_free[1]))
                        ins = v.memset(pA_s[b][:, kmax:kpad], 0.0) if em.on("v") else None
                        rv = em.inc("v", ins, SM["s_dve"])
                    em.wait("t", SM["s_dve"], rv)
                    ins = None
                    for ck in range(kpad // 512):
                        first = (qt == 4 * ck)
                        ins = t.matmul(ps[cs_banks[ck % 3]][32 * (ck // 3):32 * (ck // 3) + 1, 0:512],
                                       rinvb_s[:, 0:1], pA_s[b][:, 512 * ck:512 * ck + 512],
                                       start=first, stop=(qt == 31)) if em.on("t") else None
                    pa_free[b] = em.inc("t", ins, SM["s_mm"])
                    if qt % 4 == 3:
                        emit_decode(14)
                em.wait("v", SM["s_mm"], pa_free[b])
                for ck in range(8):
                    r0 = 32 * (ck // 3)
                    em.wait("v", SM["s_gp"])
                    ins = v.tensor_copy(tstage_s[r0:r0 + 1, 0:512],
                                        ps[cs_banks[ck % 3]][r0:r0 + 1, 0:512]) if em.on("v") else None
                    dvv = em.inc("v", ins, SM["s_dve"])
                    em.wait("q", SM["s_dve"], dvv)
                    ins = q.dma_start(out=tsum_d.ap()[ghead:ghead + 1, 512 * ck:512 * ck + 512],
                                      in_=tstage_s[r0:r0 + 1, 0:512]) if em.on("q") else None
                    em.inc("q", ins, SM["s_gp"], 16)

            def bisect_begin(p):
                em.wait("q", SM["s_gp"])
                em.wait("q", SM["s_dve"])
                ins = q.dma_start(out=bis_xf[0:1, 0:S], in_=tsum_d.ap()[2 * p:2 * p + 1, :]) if em.on("q") else None
                em.inc("q", ins, SM["s_x"], 16)
                ins = q.dma_start(out=bis_xf[64:65, 0:S], in_=tsum_d.ap()[2 * p + 1:2 * p + 2, :]) if em.on("q") else None
                lv = em.inc("q", ins, SM["s_x"], 16)
                em.wait("v", SM["s_x"], lv)
                ins = v.memset(bis_lo[:, :], 0.0) if em.on("v") else None
                em.inc("v", ins, SM["s_dve"])
                ins = v.memset(bis_hi[:, :], 4096.0) if em.on("v") else None
                em.inc("v", ins, SM["s_dve"])

            def bisect_iter():
                ins = v.tensor_add(bis_mid[:, :], bis_lo[:, :], bis_hi[:, :]) if em.on("v") else None
                em.inc("v", ins, SM["s_dve"])
                ins = v.tensor_scalar(bis_mid[:, :], bis_mid[:, :], 0.5, 0.0, ALU.mult, ALU.add) if em.on("v") else None
                em.inc("v", ins, SM["s_dve"])
                for bq in range(4):
                    ins = v.tensor_scalar(bis_scrap[:, 0:1024], bis_xf[:, bq * 1024:(bq + 1) * 1024],
                                          bis_mid[:, 0:1], 0.0, ALU.is_gt, ALU.add,
                                          accum_out=(bis_cnt if bq == 0 else bis_t)[:, 0:1]) if em.on("v") else None
                    em.inc("v", ins, SM["s_dve"])
                    if bq > 0:
                        ins = v.tensor_add(bis_cnt[:, :], bis_cnt[:, :], bis_t[:, :]) if em.on("v") else None
                        em.inc("v", ins, SM["s_dve"])
                ins = v.tensor_scalar(bis_pred[:, :], bis_cnt[:, :], HEAVY_K, 1.0,
                                      ALU.is_ge, ALU.mult) if em.on("v") else None
                em.inc("v", ins, SM["s_dve"])
                ins = v.tensor_sub(bis_d[:, :], bis_mid[:, :], bis_lo[:, :]) if em.on("v") else None
                em.inc("v", ins, SM["s_dve"])
                ins = v.tensor_mul(bis_t[:, :], bis_pred[:, :], bis_d[:, :]) if em.on("v") else None
                em.inc("v", ins, SM["s_dve"])
                ins = v.tensor_add(bis_lo[:, :], bis_lo[:, :], bis_t[:, :]) if em.on("v") else None
                em.inc("v", ins, SM["s_dve"])
                ins = v.tensor_scalar(bis_d[:, :], bis_pred[:, :], -1.0, 1.0, ALU.mult, ALU.add) if em.on("v") else None
                em.inc("v", ins, SM["s_dve"])
                ins = v.tensor_sub(bis_t[:, :], bis_mid[:, :], bis_hi[:, :]) if em.on("v") else None
                em.inc("v", ins, SM["s_dve"])
                ins = v.tensor_mul(bis_t[:, :], bis_d[:, :], bis_t[:, :]) if em.on("v") else None
                em.inc("v", ins, SM["s_dve"])
                ins = v.tensor_add(bis_hi[:, :], bis_hi[:, :], bis_t[:, :]) if em.on("v") else None
                em.inc("v", ins, SM["s_dve"])

            def bisect_end(p):
                em.wait("v", SM["s_gm"])
                gv = 0
                for hf in range(4):
                    ins = v.tensor_scalar(bis_scrap[:, 0:1024], bis_xf[:, hf * 1024:(hf + 1) * 1024],
                                          bis_lo[:, 0:1], 1.0,
                                          ALU.is_gt, ALU.mult) if em.on("v") else None
                    mk = em.inc("v", ins, SM["s_dve"])
                    em.wait("g", SM["s_dve"], mk)
                    ins = g.dma_start(out=mask_d.ap()[2 * p:2 * p + 1, hf * 1024:(hf + 1) * 1024],
                                      in_=bis_scrap[0:1, 0:1024]) if em.on("g") else None
                    em.inc("g", ins, SM["s_gm"], 16)
                    ins = g.dma_start(out=mask_d.ap()[2 * p + 1:2 * p + 2, hf * 1024:(hf + 1) * 1024],
                                      in_=bis_scrap[64:65, 0:1024]) if em.on("g") else None
                    gv = em.inc("g", ins, SM["s_gm"], 16)
                    em.wait("v", SM["s_gm"], gv)
                em.wait("q", SM["s_gm"], gv)
                lv = 0
                for hh in range(2):
                    ghd = 2 * p + hh
                    hd = ghd % 4
                    src = mask_d.ap()[ghd:ghd + 1, :].rearrange("one (c p) -> (one c) p", p=128)
                    ins = q.dma_start(out=m32_s[32 * hd:32 * hd + 32, :], in_=src) if em.on("q") else None
                    lv = em.inc("q", ins, SM["s_x"], 16)
                em.wait("t", SM["s_x"], lv)
                em.wait("t", SM["s_act"], state.get("p_trb_free", 0))
                ins = t.transpose(p_trb[:, :], m32_s, identb_s[:, :]) if em.on("t") else None
                tv2 = em.inc("t", ins, SM["s_mm"])
                em.wait("s", SM["s_mm"], tv2)
                ins = s.copy(maskT_s, p_trb[:, :]) if em.on("s") else None
                mt = em.inc("s", ins, SM["s_act"])
                state[("maskT", p)] = mt
                state["p_trb_free"] = mt

            bis_state = {}

            def bisect_step(p, budget):
                st_ = bis_state.setdefault(p, {"phase": 0, "it": 0})
                if st_["phase"] == 0:
                    bisect_begin(p)
                    st_["phase"] = 1
                while budget > 0 and st_["phase"] == 1:
                    bisect_iter()
                    st_["it"] += 1
                    budget -= 1
                    if st_["it"] >= BISECT_ITERS:
                        st_["phase"] = 2
                        bisect_end(p)

            def bisect_finish(p):
                st_ = bis_state.setdefault(p, {"phase": 0, "it": 0})
                if st_["phase"] == 0:
                    bisect_begin(p)
                    st_["phase"] = 1
                while st_["phase"] == 1:
                    bisect_iter()
                    st_["it"] += 1
                    if st_["it"] >= BISECT_ITERS:
                        st_["phase"] = 2
                        bisect_end(p)

            av_free = {0: 0, 1: 0}
            z_free = {0: 0, 1: 0}

            def pass_c(ghead, interleave=None):
                hd = ghead % 4
                em.wait("t", SM["s_act"], state[("maskT", ghead // 2)])
                em.wait("t", SM["s_dve"])
                for qc in range(8):
                    par = qc % 2
                    nkt = 4 * qc + 4
                    em.wait("t", SM["s_dve"], av_free[par])
                    em.wait("t", SM["s_dve"], z_free[par])
                    for kt in range(nkt):
                        bank = kt % 2
                        em.wait("t", SM["s_act"], state.get(("pcsc", bank), 0))
                        ins = t.matmul(ps[bank][:, 0:512], kT4[hd][:, kt * 128:(kt + 1) * 128],
                                       qT4[hd][:, qc * 512:(qc + 1) * 512], start=True, stop=True) if em.on("t") else None
                        smm = em.inc("t", ins, SM["s_mm"])
                        em.wait("s", SM["s_mm"], smm)
                        em.wait("s", SM["s_mm"], state.get(("pcPfree", bank), 0))
                        ins = s.activation(pcP_s[bank], ps[bank][:, 0:512], AF.Exp, scale=RSQ) if em.on("s") else None
                        sa = em.inc("s", ins, SM["s_act"])
                        state[("pcsc", bank)] = sa
                        em.wait("v", SM["s_act"], sa)
                        col = maskT_s[:, 32 * hd + kt:32 * hd + kt + 1]
                        if 128 * kt + 127 + 409 < 512 * qc:
                            ins = v.tensor_scalar(pcP_s[bank], pcP_s[bank], col, 0.0,
                                                  ALU.mult, ALU.add) if em.on("v") else None
                            dv2 = em.inc("v", ins, SM["s_dve"])
                        else:
                            ins = v.tensor_scalar(pcS_s[:, :], iota128c_s[:, :],
                                                  float(128 * kt - 512 * qc), 0.0,
                                                  ALU.add, ALU.add) if em.on("v") else None
                            em.inc("v", ins, SM["s_dve"])
                            ins = v.tensor_scalar(pcS410_s[:, :], pcS_s[:, :], float(WIN), 0.0,
                                                  ALU.add, ALU.add) if em.on("v") else None
                            em.inc("v", ins, SM["s_dve"])
                            ins = v.scalar_tensor_tensor(pcE_s[:, :], col, BIG, pcS410_s[:, :],
                                                         ALU.mult, ALU.add) if em.on("v") else None
                            em.inc("v", ins, SM["s_dve"])
                            ins = v.scalar_tensor_tensor(pcP_s[bank], iota512_s,
                                                         pcS_s[:, 0:1], pcP_s[bank],
                                                         ALU.is_ge, ALU.mult) if em.on("v") else None
                            em.inc("v", ins, SM["s_dve"])
                            ins = v.scalar_tensor_tensor(pcP_s[bank], iota512_s,
                                                         pcE_s[:, 0:1], pcP_s[bank],
                                                         ALU.is_lt, ALU.mult) if em.on("v") else None
                            dv2 = em.inc("v", ins, SM["s_dve"])
                        em.wait("t", SM["s_dve"], dv2)
                        ins = t.matmul(ps[4 + par][0:1, 0:512], onescb_s[:, 0:1], pcP_s[bank],
                                       start=(kt == 0), stop=(kt == nkt - 1)) if em.on("t") else None
                        em.inc("t", ins, SM["s_mm"])
                        ins = t.matmul(ps[2 + par][:, 0:512], vN4[hd][:, kt * 128:(kt + 1) * 128],
                                       pcP_s[bank], start=(kt == 0), stop=(kt == nkt - 1)) if em.on("t") else None
                        avm = em.inc("t", ins, SM["s_mm"])
                        state[("pcPfree", bank)] = avm
                    em.wait("v", SM["s_mm"])
                    ins = v.reciprocal(pcr_s[0:1, :], ps[4 + par][0:1, 0:512]) if em.on("v") else None
                    zf = em.inc("v", ins, SM["s_dve"])
                    z_free[par] = zf
                    em.wait("t", SM["s_dve"], zf)
                    em.wait("t", SM["s_act"], state.get(("pcsc", 0), 0))
                    ins = t.matmul(ps[0][:, 0:512], onesrF_s[:, :], pcr_s[0:1, :],
                                   start=True, stop=True) if em.on("t") else None
                    bm = em.inc("t", ins, SM["s_mm"])
                    em.wait("s", SM["s_mm"], bm)
                    ins = s.copy(pcrB_s, ps[0][:, 0:512]) if em.on("s") else None
                    bca = em.inc("s", ins, SM["s_act"])
                    state[("pcsc", 0)] = bca
                    em.wait("v", SM["s_act"], bca)
                    em.wait("v", SM["s_gm"])
                    ins = v.tensor_mul(pcout_s[par], ps[2 + par][:, 0:512], pcrB_s) if em.on("v") else None
                    av_free[par] = em.inc("v", ins, SM["s_dve"])
                    em.wait("g", SM["s_dve"], av_free[par])
                    r0 = ghead * 128
                    c0 = qc * 512
                    gbr(lambda r0=r0, c0=c0, par=par: g.dma_start(
                            out=sh_atA.ap()[r0:r0 + 128, c0:c0 + 512], in_=pcout_s[par]),
                        lambda r0=r0, c0=c0, par=par: g.dma_start(
                            out=sh_atB.ap()[r0:r0 + 128, c0:c0 + 512], in_=pcout_s[par]),
                        SM["s_gm"])
                    if interleave is not None:
                        interleave(qc)
                    emit_decode(60)

            for g2 in range(2):
                if STG < 2 + g2 * 3:
                    break
                proj_pair(2 * g2)
                if DBG == "qkv" and g2 == 0:
                    em.wait("g", SM["s_dve"])
                    em.wait("g", SM["s_act"])
                    em.wait("g", SM["s_mm"])
                    for i, src in enumerate([qT4[0], qT4[1], kT4[0], kT4[1]]):
                        ins = g.dma_start(out=dbgB_o[i * 128:(i + 1) * 128, :], in_=src) if em.on("g") else None
                        em.inc("g", ins, SM["s_gm"], 16)
                proj_pair(2 * g2 + 1)
                if STG < 3 + g2 * 3:
                    break
                pass_a(4 * g2 + 0)
                pass_a(4 * g2 + 1)
                bisect_step(2 * g2, 10)
                pass_a(4 * g2 + 2)
                bisect_finish(2 * g2)
                pass_a(4 * g2 + 3)
                if STG < 4 + g2 * 3:
                    break
                pass_c(4 * g2 + 0, interleave=lambda qc, p=2 * g2 + 1: bisect_step(p, 3))
                bisect_finish(2 * g2 + 1)
                pass_c(4 * g2 + 1)
                pass_c(4 * g2 + 2)
                pass_c(4 * g2 + 3)

            if STG >= 8:
                em.wait("g", SM["s_gm"])
                barrier(0)
                emit_decode(10 ** 9)
                em.wait("g", SM["s_gm"])
                barrier(1)

                def layernorm(src_f32, dst_bf, dst_cols_per_ot, t_off):
                    ins = None
                    for ot in range(NHC):
                        sl = src_f32[:, ot * 512:(ot + 1) * 512]
                        ins = v.tensor_copy(lnb_s, sl) if em.on("v") else None
                        em.inc("v", ins, SM["s_dve"])
                        em.wait("s", SM["s_dve"])
                        ins = s.activation(ln2b_s, sl, AF.Square) if em.on("s") else None
                        sa = em.inc("s", ins, SM["s_act"])
                        dvv = SM["s_dve"].n
                        em.wait("t", SM["s_dve"], dvv)
                        em.wait("t", SM["s_act"], sa)
                        ins = t.matmul(ps[4][0:1, 0:512], onescb_s[:, 0:1], lnb_s,
                                       start=(ot == 0), stop=(ot == NHC - 1)) if em.on("t") else None
                        em.inc("t", ins, SM["s_mm"])
                        ins = t.matmul(ps[5][0:1, 0:512], onescb_s[:, 0:1], ln2b_s,
                                       start=(ot == 0), stop=(ot == NHC - 1)) if em.on("t") else None
                        mmv = em.inc("t", ins, SM["s_mm"])
                        em.wait("v", SM["s_mm"], mmv)
                        em.wait("s", SM["s_mm"], mmv)
                    em.wait("v", SM["s_mm"])
                    ins = v.tensor_scalar(mean_s[0:1, :], ps[4][0:1, 0:512], 1.0 / HID, 0.0,
                                          ALU.mult, ALU.add) if em.on("v") else None
                    em.inc("v", ins, SM["s_dve"])
                    ins = v.tensor_scalar(ex2_s[0:1, :], ps[5][0:1, 0:512], 1.0 / HID, 0.0,
                                          ALU.mult, ALU.add) if em.on("v") else None
                    em.inc("v", ins, SM["s_dve"])
                    ins = v.tensor_mul(var_s[0:1, :], mean_s[0:1, :], mean_s[0:1, :]) if em.on("v") else None
                    em.inc("v", ins, SM["s_dve"])
                    ins = v.tensor_sub(var_s[0:1, :], ex2_s[0:1, :], var_s[0:1, :]) if em.on("v") else None
                    em.inc("v", ins, SM["s_dve"])
                    ins = v.tensor_scalar(var_s[0:1, :], var_s[0:1, :], float(EPS), 0.0,
                                          ALU.add, ALU.add) if em.on("v") else None
                    dvv = em.inc("v", ins, SM["s_dve"])
                    em.wait("s", SM["s_dve"], dvv)
                    ins = s.activation(sq_s[0:1, :], var_s[0:1, :], AF.Sqrt) if em.on("s") else None
                    sa = em.inc("s", ins, SM["s_act"])
                    em.wait("v", SM["s_act"], sa)
                    ins = v.reciprocal(rstd_s[0:1, :], sq_s[0:1, :]) if em.on("v") else None
                    dvv = em.inc("v", ins, SM["s_dve"])
                    em.wait("t", SM["s_dve"], dvv)
                    ins = t.matmul(ps[4][:, 0:512], onesrF_s[:, :], mean_s[0:1, :], start=True, stop=True) if em.on("t") else None
                    em.inc("t", ins, SM["s_mm"])
                    ins = t.matmul(ps[5][:, 0:512], onesrF_s[:, :], rstd_s[0:1, :], start=True, stop=True) if em.on("t") else None
                    mmv = em.inc("t", ins, SM["s_mm"])
                    em.wait("s", SM["s_mm"], mmv)
                    ins = s.copy(meanB_s, ps[4][:, 0:512]) if em.on("s") else None
                    em.inc("s", ins, SM["s_act"])
                    ins = s.copy(rstdB_s, ps[5][:, 0:512]) if em.on("s") else None
                    sa = em.inc("s", ins, SM["s_act"])
                    em.wait("v", SM["s_act"], sa)
                    for ot in range(NHC):
                        sl = src_f32[:, ot * 512:(ot + 1) * 512]
                        ins = v.tensor_sub(tmp_s, sl, meanB_s) if em.on("v") else None
                        em.inc("v", ins, SM["s_dve"])
                        if dst_bf is not None:
                            ins = v.tensor_mul(dst_bf[:, ot * dst_cols_per_ot + t_off: ot * dst_cols_per_ot + t_off + 512],
                                               tmp_s, rstdB_s) if em.on("v") else None
                            em.inc("v", ins, SM["s_dve"])
                        else:
                            ins = v.tensor_mul(sl, tmp_s, rstdB_s) if em.on("v") else None
                            dvv = em.inc("v", ins, SM["s_dve"])
                            em.wait("q", SM["s_dve"], dvv)
                            ins = q.dma_start(out=outT_o[ot * 128:(ot + 1) * 128, t_off:t_off + 512],
                                              in_=sl) if em.on("q") else None
                            em.inc("q", ins, SM["s_st"], 16)

                em.wait("t", SM["s_dve"])
                em.wait("t", SM["s_act"])
                em.wait("v", SM["s_mm"])
                em.wait("v", SM["s_act"])
                em.wait("s", SM["s_mm"])
                em.wait("s", SM["s_dve"])
                em.wait("g", SM["s_mm"])
                em.wait("q", SM["s_mm"])
                em.wait("q", SM["s_dve"])

                for tc2 in range(2):
                    em.wait("g", SM["s_mm"], state.get("atfree", 0))
                    if em.on("g"):
                        for arm in range(4):
                            c0 = arm * 1024 + tc2 * 512
                            with g.If_eq(REG["qs"], arm):
                                for hd in range(8):
                                    g.dma_start(
                                        out=at_s[:, hd * 512:(hd + 1) * 512],
                                        in_=sh_atA.ap()[hd * 128:(hd + 1) * 128, c0:c0 + 512],
                                    ).then_inc(SM["s_ga"].h, 16)
                                    g.dma_start(
                                        out=at_s[:, (8 + hd) * 512:(9 + hd) * 512],
                                        in_=sh_atB.ap()[hd * 128:(hd + 1) * 128, c0:c0 + 512],
                                    ).then_inc(SM["s_ga"].h, 16)
                    SM["s_ga"].n += 256
                    at_ld = SM["s_ga"].n
                    for og in range(4):
                        for hc in range(NHC):
                            wb = hc % 2
                            em.wait("g", SM["s_mm"], state.get(("wofree", wb), 0))
                            ins = g.dma_start(out=wo_s[wb],
                                              in_=woT[hc * 128:(hc + 1) * 128, og * 512:(og + 1) * 512]) if em.on("g") else None
                            lv = em.inc("g", ins, SM["s_w%d" % wb], 16)
                            em.wait("t", SM["s_w%d" % wb], lv)
                            em.wait("t", SM["s_ga"], at_ld)
                            ins = None
                            for oi in range(4):
                                ins = t.matmul(ps[oi][:, 0:512],
                                               wo_s[wb][:, oi * 128:oi * 128 + 128],
                                               at_s[:, hc * 512:(hc + 1) * 512],
                                               start=(hc == 0), stop=(hc == NHC - 1)) if em.on("t") else None
                            state[("wofree", wb)] = em.inc("t", ins, SM["s_mm"])
                        womm = SM["s_mm"].n
                        for oi in range(4):
                            ot = og * 4 + oi
                            xb2 = oi % 2
                            ins = q.dma_start(out=xf2[xb2],
                                              in_=xTs[ot * 128:(ot + 1) * 128, tc2 * 512:(tc2 + 1) * 512]) if em.on("q") else None
                            lv = em.inc("q", ins, SM["s_x%d" % xb2], 16)
                            em.wait("v", SM["s_x%d" % xb2], lv)
                            em.wait("v", SM["s_mm"], womm)
                            em.wait("v", SM["s_act"])
                            ins = v.tensor_add(acc_s[:, ot * 512:(ot + 1) * 512], ps[oi][:, 0:512],
                                               xf2[xb2]) if em.on("v") else None
                            dvv = em.inc("v", ins, SM["s_dve"])
                            em.wait("q", SM["s_dve"], dvv)
                            em.wait("t", SM["s_dve"], dvv)
                    layernorm(acc_s, xfb_s, 1024, tc2 * 512)
                    state["atfree"] = SM["s_mm"].n
                    em.wait("g", SM["s_dve"])

                xfb_ready = SM["s_dve"].n
                em.wait("q", SM["s_bar"], 2)
                em.wait("q", SM["s_mm"])
                for j in range(NIT):
                    wb = j % 2
                    em.wait("q", SM["s_mm"], state.get(("gufree", wb), 0))
                    src3 = sh_wg.ap()[:, j * 128:(j + 1) * 128].rearrange("(hc p) i -> p hc i", p=128)
                    ins = q.dma_start(out=gu_s[wb][:, 0:2048].rearrange("p (hc i) -> p hc i", i=128),
                                      in_=src3) if em.on("q") else None
                    em.inc("q", ins, SM["s_f%d" % wb], 16)
                    src3 = sh_wu.ap()[:, j * 128:(j + 1) * 128].rearrange("(hc p) i -> p hc i", p=128)
                    ins = q.dma_start(out=gu_s[wb][:, 2048:4096].rearrange("p (hc i) -> p hc i", i=128),
                                      in_=src3) if em.on("q") else None
                    lv = em.inc("q", ins, SM["s_f%d" % wb], 16)
                    em.wait("t", SM["s_f%d" % wb], lv)
                    em.wait("t", SM["s_dve"], xfb_ready)
                    for th in range(2):
                        em.wait("t", SM["s_dve"], state.get(("hfree",), 0))
                        ins = None
                        for hc in range(NHC):
                            ins = t.matmul(ps[0][:, 0:512], gu_s[wb][:, hc * 128: hc * 128 + 128],
                                           xfb_s[:, hc * 1024 + th * 512: hc * 1024 + th * 512 + 512],
                                           start=(hc == 0), stop=(hc == NHC - 1)) if em.on("t") else None
                        em.inc("t", ins, SM["s_mm"])
                        ins = None
                        for hc in range(NHC):
                            ins = t.matmul(ps[1][:, 0:512], gu_s[wb][:, 2048 + hc * 128: 2048 + hc * 128 + 128],
                                           xfb_s[:, hc * 1024 + th * 512: hc * 1024 + th * 512 + 512],
                                           start=(hc == 0), stop=(hc == NHC - 1)) if em.on("t") else None
                        gumm = em.inc("t", ins, SM["s_mm"])
                        state[("gufree", wb)] = gumm
                        em.wait("s", SM["s_mm"], gumm)
                        ins = s.activation(sg_s, ps[0][:, 0:512], AF.Silu) if em.on("s") else None
                        sa = em.inc("s", ins, SM["s_act"])
                        em.wait("v", SM["s_act"], sa)
                        ins = v.tensor_mul(hh_s[:, j * 1024 + th * 512: j * 1024 + th * 512 + 512],
                                           sg_s, ps[1][:, 0:512]) if em.on("v") else None
                        hdv = em.inc("v", ins, SM["s_dve"])
                        state[("hfree",)] = hdv

                em.wait("q", SM["s_mm"])
                for th in range(2):
                    for ot in range(NHC):
                        wb = ot % 2
                        em.wait("q", SM["s_mm"], state.get(("wdfree", wb), 0))
                        src3 = sh_wd.ap()[:, ot * 128:(ot + 1) * 128].rearrange("(sl p) o -> p sl o", p=128)
                        ins = q.dma_start(out=wd_s[wb][:, 0:NIT * 128].rearrange("p (sl o) -> p sl o", o=128),
                                          in_=src3) if em.on("q") else None
                        lv = em.inc("q", ins, SM["s_f%d" % wb], 16)
                        em.wait("t", SM["s_f%d" % wb], lv)
                        em.wait("t", SM["s_dve"])
                        par = ot % 2
                        em.wait("t", SM["s_dve"], state.get(("dacc", par), 0))
                        ins = None
                        for j in range(NIT):
                            ins = t.matmul(ps[2 + par][:, 0:512], wd_s[wb][:, j * 128: j * 128 + 128],
                                           hh_s[:, j * 1024 + th * 512: j * 1024 + th * 512 + 512],
                                           start=(j == 0), stop=(j == NIT - 1)) if em.on("t") else None
                        dmm = em.inc("t", ins, SM["s_mm"])
                        state[("wdfree", wb)] = dmm
                        em.wait("v", SM["s_mm"], dmm)
                        em.wait("v", SM["s_act"])
                        em.wait("v", SM["s_st"])
                        ins = v.tensor_add(acc_s[:, ot * 512:(ot + 1) * 512], ps[2 + par][:, 0:512],
                                           xfb_s[:, ot * 1024 + th * 512: ot * 1024 + th * 512 + 512]) if em.on("v") else None
                        state[("dacc", par)] = em.inc("v", ins, SM["s_dve"])
                    layernorm(acc_s, None, 0, th * 512)

                em.wait("q", SM["s_st"])
                if DBG:
                    em.wait("g", SM["s_st"])
                    em.wait("g", SM["s_dve"])
                    ins = g.dma_start(out=dbgA_o, in_=sh_atA.ap()) if em.on("g") else None
                    em.inc("g", ins, SM["s_gm"], 16)
                    ins = g.dma_start(out=dbgB_o, in_=sh_atB.ap()) if em.on("g") else None
                    em.inc("g", ins, SM["s_gm"], 16)
                    ins = g.dma_start(out=dbgX_o, in_=xfb_s) if em.on("g") else None
                    em.inc("g", ins, SM["s_gm"], 16)
            emit_decode(10 ** 9)
            if DBG in ("qkv", "rope"):
                em.wait("g", SM["s_dve"])
                em.wait("g", SM["s_act"])
                em.wait("g", SM["s_mm"])
                dump_list = [qT4[0], kT4[0], vN4[0], qT4[1], kT4[1], vN4[1],
                             AR_C[:, 0:2560]]
                for i, src in enumerate(dump_list):
                    w = src.shape[1] if hasattr(src, 'shape') else S
                    ins = g.dma_start(out=dbgA_o[i * 128:(i + 1) * 128, 0:w], in_=src) if em.on("g") else None
                    em.inc("g", ins, SM["s_gm"], 16)
            em.wait("g", SM["s_gm"])
            em.wait("g", SM["s_dk"])
            em.wait("q", SM["s_gp"])
            em.wait("q", SM["s_st"])

        @block.tensor
        def _(tt):
            schedule(Emit("t", {"t": tt, "v": None, "s": None, "g": None, "q": None}, sems))
        for sm in sems.values():
            sm.n = 0

        @block.vector
        def _(vv):
            schedule(Emit("v", {"t": None, "v": SelfSync(vv, sems["s_dve"]), "s": None, "g": None, "q": None}, sems))
        for sm in sems.values():
            sm.n = 0

        @block.scalar
        def _(ss):
            schedule(Emit("s", {"t": None, "v": None, "s": SelfSync(ss, sems["s_act"]), "g": None, "q": None}, sems))
        for sm in sems.values():
            sm.n = 0

        @block.gpsimd
        def _(gg):
            schedule(Emit("g", {"t": None, "v": None, "s": None, "g": gg, "q": None}, sems))
        for sm in sems.values():
            sm.n = 0

        @block.sync
        def _(qq):
            schedule(Emit("q", {"t": None, "v": None, "s": None, "g": None, "q": qq}, sems))

    return nc



def _dec_jobs_gu(idxT_half):
    out = np.empty((64 * 128, DECW), np.int32)
    for hc in range(16):
        for ci in range(4):
            r = (hc * 4 + ci) * 128
            out[r:r + 128, :] = idxT_half[hc * 128:(hc + 1) * 128, ci * DECW:(ci + 1) * DECW]
    return out


def _dec_jobs_d(didxT_half):
    out = np.zeros((66 * 128, DECW), np.int32)
    for rt in range(22):
        for ci, (c0, cols) in enumerate([(0, DECW), (DECW, DECW), (2 * DECW, HID - 2 * DECW)]):
            r = (rt * 3 + ci) * 128
            out[r:r + 128, 0:cols] = didxT_half[rt * 128:(rt + 1) * 128, c0:c0 + cols]
    return out

_cache = {}
_launch_counter = [0]


def kernel(L0_attn_hidden_states, L0_attn_Wq, L0_attn_Wk, L0_attn_Wv, L0_attn_Wo,
           L0_ffn_lut, L0_ffn_gate_idx, L0_ffn_up_idx, L0_ffn_down_idx):
    x = np.asarray(L0_attn_hidden_states)
    Wq = np.asarray(L0_attn_Wq)
    Wk = np.asarray(L0_attn_Wk)
    Wv = np.asarray(L0_attn_Wv)
    Wo = np.asarray(L0_attn_Wo)
    lut = np.asarray(L0_ffn_lut).astype(np.float32, copy=False)
    gidx = np.asarray(L0_ffn_gate_idx)
    uidx = np.asarray(L0_ffn_up_idx)
    didx = np.asarray(L0_ffn_down_idx)

    consts = build_consts()
    ac = np.ascontiguousarray

    xT_b = [ac(x[b].T) for b in range(2)]
    xIm_b = [ac(x[b].reshape(8, 512, 16, 128).transpose(3, 0, 2, 1).reshape(128, 65536))
             for b in range(2)]

    def wimage(W, sslot):
        return ac(W[1024 * sslot:1024 * sslot + 1024, :]
                  .reshape(4, 256, 16, 128).transpose(3, 0, 2, 1).reshape(128, 16384))

    gidxT = ac(gidx.T)
    uidxT = ac(uidx.T)
    didxT = ac(didx.T)
    woT = ac(Wo.T)

    if "fused" not in _cache:
        _cache["fused"] = build_fused()
    nc = _cache["fused"]

    _launch_counter[0] += 1
    n0 = ((int(time.time() * 1e6) ^ (_launch_counter[0] << 20)) & 0x7FFFFFFF) | 1
    n1 = ((int(time.time() * 1e3) ^ 0x5A5A5A5A ^ (_launch_counter[0] << 8)) & 0x7FFFFFFF) | 1
    nonce = np.array([[n0, n1]], np.int32)

    in_maps = []
    for c in range(8):
        sslot, gq, b = c % 2, c % 4, c // 4
        m = dict(xIm=xIm_b[b],
                 wqT=wimage(Wq, sslot),
                 wkT=wimage(Wk, sslot),
                 wvT=wimage(Wv, sslot),
                 woT=woT,
                 xTs=ac(xT_b[b][:, gq * 1024:gq * 1024 + 1024]),
                 gidxT=_dec_jobs_gu(gidxT[:, 0:DHALF] if sslot == 0 else gidxT[:, 2688:INTER]),
                 uidxT=_dec_jobs_gu(uidxT[:, 0:DHALF] if sslot == 0 else uidxT[:, 2688:INTER]),
                 didxT=_dec_jobs_d(didxT[0:DHALF, :] if sslot == 0 else didxT[2688:INTER, :]),
                 lut16=lut.reshape(1, 16),
                 myslot=np.array([[sslot]], np.int32),
                 qsel=np.array([[gq]], np.int32),
                 nonce=nonce)
        for k in CONST_KEYS:
            m[k] = consts[k]
        in_maps.append(m)
    res = run_bass_kernel_spmd(nc, in_maps, core_ids=list(range(8)))

    out = np.empty((2, S, HID), dtype=np.float32)
    for c in range(8):
        gq, b = c % 4, c // 4
        t0 = gq * 1024
        out[b, t0:t0 + 1024, :] = np.asarray(res.results[c]["outT"]).T
    return out
